# version 15
# baseline (speedup 1.0000x reference)
import numpy as np
import ml_dtypes
from contextlib import ExitStack

import concourse.bass as bass
import concourse.mybir as mybir
from concourse.bass_utils import run_bass_kernel_spmd

F32 = mybir.dt.float32
BF16 = mybir.dt.bfloat16
AF = mybir.ActivationFunctionType
ALU = mybir.AluOpType
AX = mybir.AxisListType
BF = ml_dtypes.bfloat16

Bsz, S, H = 4, 4096, 2048
NH, DK, DV = 16, 128, 128
TC = 2048
NKT = H // 128
EPS = 1e-5
CP0 = 128
CP1 = 64
NCH0 = TC // CP0
NCH1 = TC // CP1
FEXP = 1408
NPAIR = 11
NFKT = 11

ENGINES = ("t", "v", "s", "g", "y")


class _Buf:
    __slots__ = ("name", "writer", "readers")

    def __init__(self, name):
        self.name = name
        self.writer = None
        self.readers = []


class Prog:

    def __init__(self, nc, ctx):
        self.nc = nc
        self.ctx = ctx
        self.sem = {e: ctx.enter_context(nc.semaphore(f"S{e}")) for e in ENGINES}
        self.cnt = {e: 0 for e in ENGINES}
        self.dsem = {}
        self.dcnt = {}
        self.waited = {e: {} for e in ENGINES}
        self.ops = {e: [] for e in ENGINES}
        self.bufs = {}
        self.pending = {e: [] for e in ENGINES}
        self.defer = None
        self.dev_outs = []

    def buf(self, h):
        key = id(h)
        b = self.bufs.get(key)
        if b is None:
            b = _Buf(getattr(h, "name", str(key)))
            self.bufs[key] = b
        return b

    def _collect(self, eng, reads, writes):
        want = {}

        def add(t):
            if t is None:
                return
            sk, val, peng = t
            if peng == "t" and eng == "t":
                return
            if want.get(sk, 0) < val:
                want[sk] = val

        rb = [self.buf(h) for h in reads]
        wb = [self.buf(h) for h in writes]
        for b in rb:
            add(b.writer)
        for b in wb:
            add(b.writer)
            for t in b.readers:
                add(t)
        waits = []
        weng = self.waited[eng]
        for sk, val in want.items():
            if weng.get(sk, 0) < val:
                waits.append((sk, val))
                weng[sk] = val
        return waits, rb, wb

    def op(self, eng, fn, reads=(), writes=()):
        if self.defer is not None:
            self.defer.append((eng, fn, tuple(reads), tuple(writes)))
            return None
        waits, rb, wb = self._collect(eng, reads, writes)
        waits = self.pending[eng] + waits
        self.pending[eng] = []
        self.cnt[eng] += 1
        ticket = (eng, self.cnt[eng], eng)
        for b in rb:
            b.readers.append(ticket)
        for b in wb:
            b.writer = ticket
            b.readers = []
        self.ops[eng].append((waits, fn, self.sem[eng], 1))
        return ticket

    def emit_now(self, item):
        eng, fn, reads, writes = item
        waits, rb, wb = self._collect(eng, reads, writes)
        waits = self.pending[eng] + waits
        self.pending[eng] = []
        self.cnt[eng] += 1
        ticket = (eng, self.cnt[eng], eng)
        for b in rb:
            b.readers.append(ticket)
        for b in wb:
            b.writer = ticket
            b.readers = []
        self.ops[eng].append((waits, fn, self.sem[eng], 1))
        return ticket

    def dma(self, q, out, in_, reads=(), writes=(), key=None):
        assert key is not None
        if key not in self.dsem:
            self.dsem[key] = self.ctx.enter_context(self.nc.semaphore(f"D{key}"))
            self.dcnt[key] = 0
        waits, rb, wb = self._collect(q, reads, writes)
        waits = self.pending[q] + waits
        self.pending[q] = []
        self.dcnt[key] += 16
        ticket = (f"D{key}", self.dcnt[key], "dma")
        for b in rb:
            b.readers.append(ticket)
        for b in wb:
            b.writer = ticket
            b.readers = []

        def fn(eng, out=out, in_=in_):
            return eng.dma_start(out=out, in_=in_)

        self.ops[q].append((waits, fn, self.dsem[key], 16))
        return ticket

    def barrier(self):
        waits = []
        for e in ENGINES:
            if self.cnt[e]:
                waits.append((e, self.cnt[e]))
        for k, v in self.dcnt.items():
            waits.append((f"D{k}", v))
        for e in ENGINES:
            weng = self.waited[e]
            pend = []
            for sk, val in waits:
                if weng.get(sk, 0) < val:
                    pend.append((sk, val))
                    weng[sk] = val
            self.pending[e] = self.pending[e] + pend
        self.bufs = {}

    def _sem_obj(self, sk):
        if isinstance(sk, str) and len(sk) > 1 and sk.startswith("D"):
            return self.dsem[sk[1:]]
        return self.sem[sk]

    def finalize_into(self, block):
        P = self

        def runner(eng_name):
            def run(engobj):
                for waits, fn, inc_sem, inc_amt in P.ops[eng_name]:
                    for sk, val in waits:
                        engobj.wait_ge(P._sem_obj(sk), val)
                    fn(engobj).then_inc(inc_sem, inc_amt)
            return run

        block.sync(runner("y"))
        block.tensor(runner("t"))
        block.vector(runner("v"))
        block.scalar(runner("s"))
        block.gpsimd(runner("g"))

    def final_wait(self):
        self.barrier()
        waits = self.pending["y"]
        self.pending["y"] = []
        self.ops["y"].append((waits, lambda eng: eng.nop(), self.sem["y"], 1))



PHASES = ("ln1p", "p0", "ln1", "p1", "p2", "gate", "ffn")


def build_core_program(upto="ffn", dev=False):
    nc = bass.Bass("TRN2", target_bir_lowering=False)
    upto_i = PHASES.index(upto)
    scratch_kind = "ExternalOutput" if dev else "Internal"

    dr = {}

    def din(name, shape, dt):
        dr[name] = nc.dram_tensor(name, shape, dt, kind="ExternalInput")
        return dr[name]

    def dscr(name, shape, dt):
        dr[name] = nc.dram_tensor(name, shape, dt, kind=scratch_kind)
        return dr[name]

    din("xT", [H, TC], F32)
    din("xPT", [H, TC], F32)
    din("wq", [16, 128, 2048], F32)
    din("wf", [16, 128, 2048], F32)
    din("wi", [16, 128, 2048], F32)
    din("wo", [16, 128, 2048], F32)
    din("gwv", [128, 16], F32)
    din("gate", [128, 128], F32)
    din("gup", [10, NPAIR, 128, 4096], F32)
    din("dwn", [10, 128, NFKT * 2048], F32)
    din("ident", [128, 128], F32)
    din("smask", [128, 64], F32)
    din("selc", [8, 1024], F32)

    o_scr = dscr("o_scr", [H, TC], BF16)
    ao_scr = dscr("ao_scr", [H, TC], F32)
    x1_scr = dscr("x1_scr", [H, TC], F32)
    xn3_scr = dscr("xn3_scr", [H, TC], BF16)
    out = nc.dram_tensor("outT", [H, TC], F32, kind="ExternalOutput")
    dr["outT"] = out

    with ExitStack() as ctx:
        P = Prog(nc, ctx)

        _uid = [0]

        def sb(name, shape, dt, c=ctx):
            _uid[0] += 1
            return c.enter_context(nc.sbuf_tensor(f"{name}_{_uid[0]}", shape, dt))

        def ps(name, shape, dt, c=ctx):
            _uid[0] += 1
            return c.enter_context(nc.psum_tensor(f"{name}_{_uid[0]}", shape, dt))

        def dump(name, src_sb, shape, dt=F32):
            if not dev:
                return
            d = nc.dram_tensor(f"dbg_{name}", shape, dt, kind="ExternalOutput")
            P.dma("y", d.ap(), src_sb, reads=[src_sb.tensor if hasattr(src_sb, "tensor") else src_sb],
                  key=f"dbg_{name}")

        ones128 = sb("ones128", [128, 128], BF16)
        idb = sb("idb", [128, 128], BF16)
        idf = sb("idf", [128, 128], F32)
        smk = sb("smk", [128, 64], BF16)
        gwvb = sb("gwvb", [128, 16], F32)

        P.dma("g", idb[:], dr["ident"].ap(), writes=[idb], key="consts")
        P.dma("y", idf[:], dr["ident"].ap(), writes=[idf], key="c2")
        P.dma("g", smk[:], dr["smask"].ap(), writes=[smk], key="consts")
        P.dma("y", gwvb[:], dr["gwv"].ap(), writes=[gwvb], key="c2")
        P.op("v", lambda v: v.memset(ones128[:], 1.0), writes=[ones128])
        scanenv = {}

        def emit_ln_to_bf16(tag, src_name, dst_sb, c):
            src_dram = dr[src_name]
            pSum = ps(f"pSum_{tag}", [128, 2048], F32, c)
            pSq = ps(f"pSq_{tag}", [128, 2048], F32, c)
            xsl = [sb(f"xsl_{tag}{i}", [128, 2048], F32, c) for i in range(2)]
            xbsl = [sb(f"xb_{tag}{i}", [128, 2048], BF16, c) for i in range(2)]
            xsq = [sb(f"xq_{tag}{i}", [128, 2048], BF16, c) for i in range(2)]
            musb = sb(f"mu_{tag}", [128, 2048], F32, c)
            rsb = sb(f"r_{tag}", [128, 2048], F32, c)
            tmp = sb(f"tmp_{tag}", [128, 2048], F32, c)
            t1 = [sb(f"t1_{tag}{i}", [128, 512], F32, c) for i in range(2)]

            for kt in range(NKT):
                sl, xb, xq = xsl[kt % 2], xbsl[kt % 2], xsq[kt % 2]
                P.dma("y", sl[:], src_dram.ap()[kt * 128:(kt + 1) * 128, :],
                      reads=[src_dram], writes=[sl], key=f"x{tag}{kt % 2}")
                P.op("g", lambda g, sl=sl, xb=xb: g.tensor_copy(xb[:], sl[:]),
                     reads=[sl], writes=[xb])
                P.op("s", lambda s, xb=xb, xq=xq: s.activation(xq[:], xb[:], AF.Square),
                     reads=[xb], writes=[xq])
                for ch in range(4):
                    c0 = ch * 512
                    P.op("t", lambda t, kt=kt, c0=c0, xb=xb: t.matmul(
                        pSum[:, c0:c0 + 512], ones128[:], xb[:, c0:c0 + 512],
                        start=(kt == 0), stop=(kt == NKT - 1)),
                        reads=[ones128, xb], writes=[pSum])
                    P.op("t", lambda t, kt=kt, c0=c0, xq=xq: t.matmul(
                        pSq[:, c0:c0 + 512], ones128[:], xq[:, c0:c0 + 512],
                        start=(kt == 0), stop=(kt == NKT - 1)),
                        reads=[ones128, xq], writes=[pSq])
            P.op("v", lambda v: v.tensor_scalar(musb[:], pSum[:], 1.0 / H, None, ALU.mult),
                 reads=[pSum], writes=[musb])
            P.op("s", lambda s: s.activation(tmp[:], musb[:], AF.Square),
                 reads=[musb], writes=[tmp])
            P.op("v", lambda v: v.scalar_tensor_tensor(tmp[:], pSq[:], 1.0 / H, tmp[:],
                                                       ALU.mult, ALU.subtract),
                 reads=[pSq, tmp], writes=[tmp])
            P.op("v", lambda v: v.tensor_scalar(tmp[:], tmp[:], float(EPS), None, ALU.add),
                 reads=[tmp], writes=[tmp])
            P.op("s", lambda s: s.activation(tmp[:], tmp[:], AF.Ln),
                 reads=[tmp], writes=[tmp])
            P.op("s", lambda s: s.activation(rsb[:], tmp[:], AF.Exp, scale=-0.5),
                 reads=[tmp], writes=[rsb])
            for kt in range(NKT):
                sl = xsl[kt % 2]
                P.dma("y", sl[:], src_dram.ap()[kt * 128:(kt + 1) * 128, :],
                      reads=[src_dram], writes=[sl], key=f"x{tag}{kt % 2}")
                for ch in range(4):
                    c0 = ch * 512
                    tb = t1[ch % 2]
                    P.op("v", lambda v, sl=sl, c0=c0, tb=tb: v.tensor_tensor(
                        tb[:], sl[:, c0:c0 + 512], musb[:, c0:c0 + 512], ALU.subtract),
                        reads=[sl, musb], writes=[tb])
                    P.op("g", lambda g, kt=kt, c0=c0, tb=tb: g.tensor_tensor(
                        dst_sb[:, kt * 2048 + c0: kt * 2048 + c0 + 512],
                        tb[:], rsb[:, c0:c0 + 512], ALU.mult),
                        reads=[tb, rsb], writes=[dst_sb])

        def emit_scan_phase(main, c):
            xnb = scanenv["xnb"]
            h0b = scanenv["h0b"]
            onesf = scanenv["onesf"]
            C = CP0 if not main else CP1
            NCH = NCH0 if not main else NCH1
            NPC = NCH // 2 if main else NCH

            pPJ = [ps(f"pPJ{i}", [128, 512], F32, c) for i in range(2)]
            pU = [ps(f"pU{i}", [128, 512], F32, c) for i in range(2)]
            pTr = [ps(f"pTr{i}", [128, 1024], BF16, c) for i in range(2)]
            if main:
                pS = [ps(f"pS{i}", [128, 512], F32, c) for i in range(2)]
                pO = pS[0]

            wnames = ("wq", "wf", "wi") if main else ("wf", "wi")
            wbuf = {n: [sb(f"wb_{n}{i}", [128, 2048], BF16, c) for i in range(2)]
                    for n in wnames}

            fsb = [sb(f"fsb{i}", [128, 2048], F32, c) for i in range(2)]
            gsb = [sb(f"gsb{i}", [128, 2048], F32, c) for i in range(2)]
            esb1 = sb("esb", [128, 2048], F32, c)
            esb = [esb1, esb1]
            tmps = [sb(f"tmps{i}", [128, 2048], F32, c) for i in range(2)]
            ksb = sb("ksb", [128, 2048], BF16, c)
            vsb = [sb(f"vsb{i}", [128, 2048], BF16, c) for i in range(2)]
            khat = [sb(f"khat{i}", [128, 2048], BF16, c) for i in range(2)]
            eD = [sb(f"eD{i}", [128, 32], F32, c) for i in range(2)]
            Bc = sb("Bc", [128, 32], F32, c)
            Bp = sb("Bp", [128, 32], F32, c)
            vt_all = sb("vt_all", [128, 2048], BF16, c)
            ovtags = [[object() for _ in range(5)] for _ in range(2)]
            hB = sb("hB", [128, 128], F32, c)
            if main:
                qb = [sb(f"qb{i}", [128, 2048], BF16, c) for i in range(2)]
                qtl = [sb(f"qtl{i}", [128, 2048], BF16, c) for i in range(2)]
                Hst = sb("Hst", [128, NCH1 * 128], BF16, c)

            def load_w(n, h):
                P.dma("g", wbuf[n][h % 2][:], dr[n].ap()[h],
                      reads=[dr[n]], writes=[wbuf[n][h % 2]], key=f"w{n}{h % 2}")

            for n in wnames:
                load_w(n, 0)
                load_w(n, 1)

            def proj(wn, h, evac_dst):
                wb = wbuf[wn][h % 2]
                for grp in range(4):
                    pt = pPJ[grp % 2]
                    t0 = grp * 512
                    for kt in range(NKT):
                        P.op("t", lambda t, kt=kt, t0=t0, wb=wb, pt=pt: t.matmul(
                            pt[:], wb[:, kt * 128:(kt + 1) * 128],
                            xnb[:, kt * 2048 + t0: kt * 2048 + t0 + 512],
                            start=(kt == 0), stop=(kt == NKT - 1)),
                            reads=[wb, xnb], writes=[pt])
                    if grp % 2 == 0:
                        P.op("s", lambda s, pt=pt, t0=t0: s.copy(
                            evac_dst[:, t0:t0 + 512], pt[:]),
                            reads=[pt], writes=[evac_dst])
                    else:
                        P.op("v", lambda v, pt=pt, t0=t0: v.tensor_copy(
                            evac_dst[:, t0:t0 + 512], pt[:]),
                            reads=[pt], writes=[evac_dst])
                if h + 2 < NH:
                    load_w(wn, h + 2)

            def logsig(dst, src, hb, extra_writes=()):
                tp, ep = tmps[hb], esb[hb]
                P.op("v", lambda v: v.scalar_tensor_tensor(tp[:], src[:], -1.0, src[:],
                                                           ALU.mult, ALU.max),
                     reads=[src], writes=[tp] + list(extra_writes))
                P.op("s", lambda s: s.activation(ep[:], tp[:], AF.Exp, scale=-1.0),
                     reads=[tp], writes=[ep])
                P.op("s", lambda s: s.activation(ep[:], ep[:], AF.Ln, bias=1.0),
                     reads=[ep], writes=[ep])
                P.op("v", lambda v: v.scalar_tensor_tensor(dst[:], src[:], 0.0, ep[:],
                                                           ALU.min, ALU.subtract),
                     reads=[src, ep], writes=[dst])

            def emit_proj_elem(h, proj_now=True):
                hb = h % 2
                fs, gs, ep, tp = fsb[hb], gsb[hb], esb[hb], tmps[hb]
                Aw = fs
                subc = gs
                if proj_now:
                    proj("wf", h, fs)
                    proj("wi", h, vsb[hb])
                    if main:
                        proj("wq", h, qb[hb])
                    return
                P.op("s", lambda s: s.activation(ep[:], fs[:], AF.Sigmoid),
                     reads=[fs], writes=([ep] + list(ovtags[hb])))
                if main:
                    P.op("s", lambda s, hb=hb: s.activation(qb[hb][:], qb[hb][:],
                                                            AF.Silu),
                         reads=[qb[hb]], writes=[qb[hb]])
                P.op("s", lambda s: s.activation(gs[:], ep[:], AF.Ln),
                     reads=[ep], writes=[gs])
                P.op("v", lambda v: v.tensor_scalar(ksb[:], ep[:], -1.0, 1.0,
                                                    ALU.mult, ALU.add),
                     reads=[ep], writes=[ksb])
                P.op("v", lambda v: v.tensor_tensor_scan(Aw[:], onesf[:], gs[:], 0.0,
                                                         ALU.mult, ALU.add),
                     reads=[onesf, gs], writes=[Aw])
                P.op("v", lambda v: v.tensor_copy(Bc[:, 0:NCH], Aw[:, C - 1::C]),
                     reads=[Aw], writes=[Bc])
                P.op("v", lambda v: v.memset(Bp[:, 0:1], 0.0), writes=[Bp])
                P.op("v", lambda v: v.tensor_copy(Bp[:, 1:NCH], Bc[:, 0:NCH - 1]),
                     reads=[Bc], writes=[Bp])
                P.op("v", lambda v, hb=hb: v.tensor_tensor(eD[hb][:, 0:NCH], Bc[:, 0:NCH],
                                                           Bp[:, 0:NCH], ALU.subtract),
                     reads=[Bc, Bp], writes=[eD[hb]])
                P.op("s", lambda s, hb=hb: s.activation(eD[hb][:, 0:NCH], eD[hb][:, 0:NCH],
                                                        AF.Exp),
                     reads=[eD[hb]], writes=[eD[hb]])
                if not main:
                    for cch in range(NCH):
                        P.op("v", lambda v, cch=cch: v.tensor_scalar(
                            tp[:, cch * C:(cch + 1) * C], Aw[:, cch * C:(cch + 1) * C],
                            Bc[:, cch:cch + 1], None, ALU.subtract),
                            reads=[Aw, Bc], writes=[tp])
                    P.op("s", lambda s: s.activation(ep[:], tp[:], AF.Exp, scale=-1.0),
                         reads=[tp], writes=[ep])
                    P.op("g", lambda g, hb=hb: g.tensor_tensor(khat[hb][:], ksb[:], ep[:],
                                                               ALU.mult),
                         reads=[ksb, ep], writes=[khat[hb]])
                else:
                    for cch in range(NCH):
                        P.op("v", lambda v, cch=cch: v.tensor_scalar(
                            subc[:, cch * C:(cch + 1) * C], Aw[:, cch * C:(cch + 1) * C],
                            Bp[:, cch:cch + 1], None, ALU.subtract),
                            reads=[Aw, Bp], writes=[subc])
                    P.op("s", lambda s: s.activation(tp[:], subc[:], AF.Exp),
                         reads=[subc], writes=[tp])
                    P.op("v", lambda v, hb=hb: v.scalar_tensor_tensor(
                        qtl[hb][:], qb[hb][:], float(DK ** -0.5), tp[:],
                        ALU.mult, ALU.mult),
                        reads=[qb[hb], tp], writes=[qtl[hb]])
                    P.op("v", lambda v: v.tensor_scalar(subc[:], subc[:], -80.0, None,
                                                        ALU.max),
                         reads=[subc], writes=[subc])
                    P.op("s", lambda s: s.activation(ep[:], subc[:], AF.Exp, scale=-1.0),
                         reads=[subc], writes=[ep])
                    P.op("g", lambda g, hb=hb: g.tensor_tensor(khat[hb][:], ksb[:],
                                                               ep[:], ALU.mult),
                         reads=[ksb, ep], writes=[khat[hb]])

            def emit_ktl(h):
                hb = h % 2
                tbits = tmps[hb][:].bitcast(BF16)
                ktlv = tbits[:, 2048:4096]
                tagKL = ovtags[hb][4]
                for cch in range(NCH):
                    P.op("v", lambda v, cch=cch, hb=hb, ktlv=ktlv: v.tensor_scalar(
                        ktlv[:, cch * C:(cch + 1) * C],
                        khat[hb][:, cch * C:(cch + 1) * C],
                        eD[hb][:, cch:cch + 1], None, ALU.mult),
                        reads=[khat[hb], eD[hb]], writes=[tagKL])

            def emit_scan_part(h, pump=lambda: None):
                hb = h % 2
                hsl = h0b[:, h * 128:(h + 1) * 128]
                tpb = tmps[hb]
                tbits = tpb[:].bitcast(BF16)
                hupd = tpb[:, 0:128]
                ktv = [tbits[:, 256:384], tbits[:, 384:512]]
                ostg = tbits[:, 512:1024]
                sm_all = tbits[:, 1024:2048] if main else None
                ktlv = tbits[:, 2048:4096]
                tagK, tagS, tagO, tagH, tagKL = ovtags[hb]
                if main:
                    P.op("g", lambda g, hsl=hsl: g.tensor_copy(Hst[:, 0:128], hsl),
                         reads=[h0b], writes=[Hst])
                else:
                    P.op("v", lambda v, hsl=hsl: v.memset(hsl, 0.0), writes=[h0b])
                def emit_pair_transposes(p):
                    c0 = p * 128
                    pt = pTr[p % 2]
                    P.op("t", lambda t, c0=c0, pt=pt, hb=hb: t.transpose(
                        pt[:, 0:128], vsb[hb][:, c0:c0 + 128], idb[:]),
                        reads=[vsb[hb], idb], writes=[pt])
                    ktsrc = ktlv[:, c0:c0 + 128] if main else khat[hb][:, c0:c0 + 128]
                    ktr = tagKL if main else khat[hb]
                    P.op("t", lambda t, pt=pt, ktsrc=ktsrc: t.transpose(
                        pt[:, 128:256], ktsrc, idb[:]),
                        reads=[ktr, idb], writes=[pt])
                    P.op("s", lambda s, c0=c0, pt=pt: s.copy(
                        vt_all[:, c0:c0 + 128], pt[:, 0:128]),
                        reads=[pt], writes=[vt_all])
                    P.op("s", lambda s, p=p, pt=pt: s.copy(
                        ktv[p % 2], pt[:, 128:256]),
                        reads=[pt], writes=[tagK])

                def emit_pair_su(p):
                    if main:
                        psx = pS[p % 2]
                        for sc_ in range(2):
                            cch = p * 2 + sc_
                            cc0 = cch * C
                            b0 = sc_ * 64
                            P.op("t", lambda t, cc0=cc0, b0=b0, psx=psx, hb=hb: t.matmul(
                                psx[b0:b0 + 64, 0:64], khat[hb][:, cc0:cc0 + 64],
                                qtl[hb][:, cc0:cc0 + 64], start=True, stop=True),
                                reads=[khat[hb], qtl[hb]], writes=[psx])
                        P.op("v", lambda v, p=p, psx=psx: v.tensor_tensor(
                            sm_all[:, p * 64:(p + 1) * 64], psx[:, 0:64],
                            smk[:], ALU.mult),
                            reads=[psx, smk], writes=[tagS])
                    if main:
                        for sc_ in range(2):
                            cch = p * 2 + sc_
                            pux = pU[cch % 2]
                            b0 = sc_ * 64
                            P.op("t", lambda t, p=p, b0=b0, pux=pux: t.matmul(
                                pux[:, 0:128],
                                ktv[p % 2][b0:b0 + 64, :],
                                vt_all[b0:b0 + 64, p * 128:(p + 1) * 128],
                                start=True, stop=True),
                                reads=[tagK, vt_all], writes=[pux])
                            hsrc = hsl if cch % 2 == 0 else hB[:]
                            hdst = hB[:] if cch % 2 == 0 else hsl
                            rsrc = h0b if cch % 2 == 0 else hB
                            rdst = hB if cch % 2 == 0 else h0b
                            P.op("v", lambda v, hsrc=hsrc, hdst=hdst, cch=cch, hb=hb, pux=pux:
                                 v.scalar_tensor_tensor(hdst, hsrc,
                                                        eD[hb][:, cch:cch + 1],
                                                        pux[:, 0:128], ALU.mult, ALU.add),
                                 reads=[rsrc, eD[hb], pux], writes=[rdst])
                            if cch + 1 < NCH:
                                P.op("g", lambda g, hdst=hdst, cch=cch: g.tensor_copy(
                                    Hst[:, (cch + 1) * 128:(cch + 2) * 128], hdst),
                                    reads=[rdst], writes=[Hst])
                    else:
                        cch = p
                        pux = pU[cch % 2]
                        P.op("t", lambda t, p=p, pux=pux: t.matmul(
                            pux[:, 0:128], ktv[p % 2],
                            vt_all[:, p * 128:(p + 1) * 128],
                            start=True, stop=True),
                            reads=[tagK, vt_all], writes=[pux])
                        P.op("v", lambda v, hsl=hsl, cch=cch, pux=pux, hb=hb: v.scalar_tensor_tensor(
                            hsl, hsl, eD[hb][:, cch:cch + 1], pux[:, 0:128],
                            ALU.mult, ALU.add),
                            reads=[h0b, eD[hb], pux], writes=[h0b])

                emit_pair_transposes(0)
                for p in range(NPC):
                    if p + 1 < NPC:
                        emit_pair_transposes(p + 1)
                    emit_pair_su(p)
                    pump()
                if main:
                    for cch in range(NCH):
                        cc0 = cch * C
                        oslot = (cch % 8) * 64
                        pr = cch // 2
                        b0 = (cch % 2) * 64
                        P.op("t", lambda t, cch=cch, cc0=cc0, oslot=oslot, hb=hb: t.matmul(
                            pO[:, oslot:oslot + 64], Hst[:, cch * 128:(cch + 1) * 128],
                            qtl[hb][:, cc0:cc0 + 64], start=True, stop=False),
                            reads=[Hst, qtl[hb]], writes=[pO])
                        P.op("t", lambda t, b0=b0, pr=pr, oslot=oslot: t.matmul(
                            pO[:, oslot:oslot + 64], vt_all[b0:b0 + 64, pr * 128:(pr + 1) * 128],
                            sm_all[b0:b0 + 64, pr * 64:(pr + 1) * 64],
                            start=False, stop=True),
                            reads=[vt_all, tagS], writes=[pO])
                        if cch % 8 == 7:
                            e0 = (cch // 8) * 512
                            P.op("s", lambda s: s.copy(ostg, pO[:]),
                                 reads=[pO], writes=[tagO])
                            P.dma("s", o_scr.ap()[h * 128:(h + 1) * 128, e0:e0 + 512],
                                  ostg, reads=[tagO], writes=[o_scr], key="ost")

            from collections import deque

            def build_chain(h):
                P.defer = []
                emit_proj_elem(h, proj_now=False)
                if main:
                    emit_ktl(h)
                lst = P.defer
                P.defer = None
                return deque(lst)

            def flush(q, k=None):
                n = len(q) if k is None else min(k, len(q))
                for _ in range(n):
                    P.emit_now(q.popleft())

            emit_proj_elem(0, proj_now=True)
            q0 = build_chain(0)
            flush(q0)
            emit_proj_elem(1, proj_now=True)
            chainq = build_chain(1)
            flush(chainq, 6)
            for h in range(NH):
                emit_scan_part(h, lambda: flush(chainq, 8))
                flush(chainq)
                if h + 2 < NH:
                    emit_proj_elem(h + 2, proj_now=True)
                    chainq = build_chain(h + 2)
            if dev:
                dump("h0b" + ("m" if main else "p"), h0b[:], [128, NH * 128])

        def emit_p2(c):
            pStA = ps("pStA", [128, 2048], F32, c)
            pW = [ps(f"pW{i}", [128, 512], F32, c) for i in range(2)]
            pStB = ps("pStB", [128, 512], F32, c)
            pStC = ps("pStC", [128, 512], F32, c)
            osq1 = sb("osq", [128, 2048], BF16, c)
            osq2 = [osq1, osq1]
            obf_all = sb("obf_all", [128, NKT * 2048], BF16, c)
            wob = sb("wob", [128, NKT * 2048], BF16, c)
            r2 = sb("r2", [128, 2048], F32, c)
            tmp = sb("tmp2", [128, 2048], F32, c)
            musb = sb("mu2", [128, 512], F32, c)
            rsb = sb("rln2", [128, 512], F32, c)
            mu3 = sb("mu3", [128, 512], F32, c)
            r3 = sb("r3", [128, 512], F32, c)
            aosl = [sb(f"aosl{i}", [128, 512], F32, c) for i in range(2)]
            wsl = [sb(f"wsl{i}", [128, 512], F32, c) for i in range(2)]
            aob = [sb(f"aob{i}", [128, 512], BF16, c) for i in range(2)]
            asq = [sb(f"asq{i}", [128, 512], BF16, c) for i in range(2)]
            x1sl = [sb(f"x1sl{i}", [128, 512], F32, c) for i in range(2)]
            x1b_all = sb("x1b_all", [128, NKT * 512], BF16, c)
            xsq = [sb(f"xsq{i}", [128, 512], BF16, c) for i in range(2)]
            t1l = [sb(f"t1b{i}", [128, 512], F32, c) for i in range(2)]
            t1 = t1l[0]
            t2 = [sb(f"t2b{i}", [128, 512], F32, c) for i in range(2)]
            xnb3 = [sb(f"xnb3{i}", [128, 512], BF16, c) for i in range(2)]
            mu2p = sb("mu2p", [128, 512], F32, c)
            rs2p = sb("rs2p", [128, 512], F32, c)

            wotag = [object() for _ in range(4)]
            obtag = [object() for _ in range(4)]
            for kt in range(NKT):
                P.dma("g", wob[:, kt * 2048:(kt + 1) * 2048], dr["wo"].ap()[kt],
                      reads=[dr["wo"]], writes=[wotag[kt % 4]], key=f"wo{kt % 4}")
                P.op("g", lambda g, kt=kt: g.tensor_scalar(
                    wob[:, kt * 2048:(kt + 1) * 2048],
                    wob[:, kt * 2048:(kt + 1) * 2048],
                    gwvb[:, kt:kt + 1], None, ALU.mult),
                    reads=[wotag[kt % 4], gwvb], writes=[wotag[kt % 4]])
            for kt in range(NKT):
                osl_ = obf_all[:, kt * 2048:(kt + 1) * 2048]
                oq = osq2[kt % 2]
                P.dma("y", osl_, o_scr.ap()[kt * 128:(kt + 1) * 128, :],
                      reads=[o_scr], writes=[obtag[kt % 4]], key=f"ord{kt % 4}")
                P.op("s", lambda s, osl_=osl_, oq=oq, kt=kt: s.activation(oq[:], osl_, AF.Square),
                     reads=[obtag[kt % 4]], writes=[oq])
                for ch in range(4):
                    c0 = ch * 512
                    P.op("t", lambda t, kt=kt, c0=c0, oq=oq: t.matmul(
                        pStA[:, c0:c0 + 512], ones128[:], oq[:, c0:c0 + 512],
                        start=(kt == 0), stop=(kt == NKT - 1)),
                        reads=[ones128, oq], writes=[pStA])
            P.op("v", lambda v: v.tensor_scalar(tmp[:], pStA[:], 1.0 / H, None, ALU.mult),
                 reads=[pStA], writes=[tmp])
            P.op("v", lambda v: v.tensor_scalar(tmp[:], tmp[:], float(EPS), None, ALU.add),
                 reads=[tmp], writes=[tmp])
            P.op("s", lambda s: s.activation(tmp[:], tmp[:], AF.Ln),
                 reads=[tmp], writes=[tmp])
            P.op("s", lambda s: s.activation(r2[:], tmp[:], AF.Exp, scale=-0.5),
                 reads=[tmp], writes=[r2])

            def stats_to_mur(mu_dst, r_dst, rpre=None, srcB=None, srcC=None,
                             srcT=()):
                sB = pStB[:] if srcB is None else srcB
                sC = pStC[:] if srcC is None else srcC
                rT = list(srcT) if srcT else ([pStB, pStC] if srcB is None else [])
                P.op("v", lambda v: v.tensor_scalar(mu_dst[:], sB, 1.0 / H,
                                                    None, ALU.mult),
                     reads=[rT[0]], writes=[mu_dst])
                if rpre is not None:
                    P.op("v", lambda v: v.tensor_tensor(mu_dst[:], mu_dst[:], rpre,
                                                        ALU.mult),
                         reads=[mu_dst], writes=[mu_dst])
                P.op("s", lambda s: s.activation(t1[:], mu_dst[:], AF.Square),
                     reads=[mu_dst], writes=[t1])
                P.op("v", lambda v: v.scalar_tensor_tensor(t1[:], sC, 1.0 / H,
                                                           t1[:], ALU.mult,
                                                           ALU.subtract)
                     if rpre is None else
                     v.tensor_scalar(t2[0][:], sC, 1.0 / H, None, ALU.mult),
                     reads=[rT[-1], t1], writes=[t1] if rpre is None else [t2[0]])
                if rpre is not None:
                    P.op("v", lambda v: v.tensor_tensor(t2[0][:], t2[0][:], rpre, ALU.mult),
                         reads=[t2[0]], writes=[t2[0]])
                    P.op("v", lambda v: v.tensor_tensor(t2[0][:], t2[0][:], rpre, ALU.mult),
                         reads=[t2[0]], writes=[t2[0]])
                    P.op("v", lambda v: v.tensor_tensor(t1[:], t2[0][:], t1[:],
                                                        ALU.subtract),
                         reads=[t1, t2[0]], writes=[t1])
                P.op("v", lambda v: v.tensor_scalar(t1[:], t1[:], float(EPS), None,
                                                    ALU.add),
                     reads=[t1], writes=[t1])
                P.op("s", lambda s: s.activation(t1[:], t1[:], AF.Ln),
                     reads=[t1], writes=[t1])
                P.op("s", lambda s: s.activation(r_dst[:], t1[:], AF.Exp, scale=-0.5),
                     reads=[t1], writes=[r_dst])

            def wo_part(ch):
                c0 = ch * 512
                for mt in range(NKT + 1):
                    if mt < NKT:
                        pm = pW[mt % 2]
                        for kt in range(NKT):
                            P.op("t", lambda t, pm=pm, mt=mt, kt=kt, c0=c0: t.matmul(
                                pm[:], wob[:, kt * 2048 + mt * 128: kt * 2048 + (mt + 1) * 128],
                                obf_all[:, kt * 2048 + c0: kt * 2048 + c0 + 512],
                                start=(kt == 0), stop=(kt == NKT - 1)),
                                reads=[wotag[kt % 4], obtag[kt % 4]], writes=[pm])
                        asl = wsl[mt % 2]
                        P.op("s", lambda s, pm=pm, asl=asl: s.copy(asl[:], pm[:]),
                             reads=[pm], writes=[asl])
                        P.dma("s", ao_scr.ap()[mt * 128:(mt + 1) * 128, c0:c0 + 512],
                              asl[:], reads=[asl], writes=[ao_scr], key="aost")
                        P.op("g", lambda g, mt=mt, asl=asl: g.tensor_copy(
                            aob[mt % 2][:], asl[:]),
                            reads=[asl], writes=[aob[mt % 2]])
                        P.op("s", lambda s, mt=mt: s.activation(
                            asq[mt % 2][:], aob[mt % 2][:], AF.Square),
                            reads=[aob[mt % 2]], writes=[asq[mt % 2]])
                    if mt > 0:
                        q = mt - 1
                        P.op("t", lambda t, q=q: t.matmul(
                            pStB[:], ones128[:], aob[q % 2][:], start=(q == 0),
                            stop=(q == NKT - 1)),
                            reads=[ones128, aob[q % 2]], writes=[pStB])
                        P.op("t", lambda t, q=q: t.matmul(
                            pStC[:], ones128[:], asq[q % 2][:], start=(q == 0),
                            stop=(q == NKT - 1)),
                            reads=[ones128, asq[q % 2]], writes=[pStC])

            def x1_part(ch):
                c0 = ch * 512
                r2c = r2[:, c0:c0 + 512]
                P.op("v", lambda v, r2c=r2c: v.reciprocal(t2[0][:], r2c),
                     reads=[r2], writes=[t2[0]])
                P.op("v", lambda v: v.tensor_tensor(mu2p[:], musb[:], t2[0][:],
                                                    ALU.mult),
                     reads=[musb, t2[0]], writes=[mu2p])
                P.op("v", lambda v, r2c=r2c: v.tensor_tensor(rs2p[:], rsb[:], r2c,
                                                             ALU.mult),
                     reads=[rsb, r2], writes=[rs2p])
                for mt in range(NKT + 1):
                    if mt < NKT:
                        asl, xsl_, tb = aosl[mt % 2], x1sl[mt % 2], t2[mt % 2]
                        xb = x1b_all[:, mt * 512:(mt + 1) * 512]
                        P.dma("y", asl[:], ao_scr.ap()[mt * 128:(mt + 1) * 128,
                                                       c0:c0 + 512],
                              reads=[ao_scr], writes=[asl], key=f"aord{mt % 2}")
                        P.dma("y", xsl_[:], dr["xT"].ap()[mt * 128:(mt + 1) * 128,
                                                          c0:c0 + 512],
                              reads=[dr["xT"]], writes=[xsl_], key=f"xrd{mt % 2}")
                        ta = t1l[mt % 2]
                        P.op("v", lambda v, asl=asl, ta=ta: v.tensor_tensor(
                            ta[:], asl[:], mu2p[:], ALU.subtract),
                            reads=[asl, mu2p], writes=[ta])
                        P.op("g", lambda g, tb=tb, ta=ta: g.tensor_tensor(tb[:], ta[:],
                                                                          rs2p[:],
                                                                          ALU.mult),
                             reads=[ta, rs2p], writes=[tb])
                        P.op("v", lambda v, xsl_=xsl_, tb=tb: v.tensor_tensor(
                            xsl_[:], xsl_[:], tb[:], ALU.add),
                            reads=[tb, xsl_], writes=[xsl_])
                        P.dma("s", x1_scr.ap()[mt * 128:(mt + 1) * 128, c0:c0 + 512],
                              xsl_[:], reads=[xsl_], writes=[x1_scr], key="x1st")
                        P.op("g", lambda g, xb=xb, xsl_=xsl_: g.tensor_copy(xb, xsl_[:]),
                             reads=[xsl_], writes=[x1b_all])
                        P.op("s", lambda s, mt=mt, xb=xb: s.activation(
                            xsq[mt % 2][:], xb, AF.Square),
                            reads=[x1b_all], writes=[xsq[mt % 2]])
                    if mt > 0:
                        q = mt - 1
                        xbq = x1b_all[:, q * 512:(q + 1) * 512]
                        P.op("t", lambda t, q=q, xbq=xbq: t.matmul(
                            pStA[:, 0:512], ones128[:], xbq, start=(q == 0),
                            stop=(q == NKT - 1)),
                            reads=[ones128, x1b_all], writes=[pStA])
                        P.op("t", lambda t, q=q: t.matmul(
                            pStA[:, 512:1024], ones128[:], xsq[q % 2][:], start=(q == 0),
                            stop=(q == NKT - 1)),
                            reads=[ones128, xsq[q % 2]], writes=[pStA])
                stats_to_mur(mu3, r3, srcB=pStA[:, 0:512], srcC=pStA[:, 512:1024],
                             srcT=[pStA, pStA])
                for mt in range(NKT):
                    xb = x1b_all[:, mt * 512:(mt + 1) * 512]
                    ab = xnb3[mt % 2]
                    ta = t1l[mt % 2]
                    P.op("v", lambda v, xb=xb, ta=ta: v.tensor_tensor(ta[:], xb, mu3[:],
                                                               ALU.subtract),
                         reads=[x1b_all, mu3], writes=[ta])
                    P.op("g", lambda g, ab=ab, ta=ta: g.tensor_tensor(ab[:], ta[:],
                                                               r3[:], ALU.mult),
                         reads=[ta, r3], writes=[ab])
                    P.dma("s", xn3_scr.ap()[mt * 128:(mt + 1) * 128, c0:c0 + 512],
                          ab[:], reads=[ab], writes=[xn3_scr], key="xn3st")

            for ch in range(5):
                if ch >= 1:
                    stats_to_mur(musb, rsb, rpre=r2[:, (ch - 1) * 512:ch * 512])
                if ch < 4:
                    wo_part(ch)
                if ch >= 1:
                    x1_part(ch - 1)

        def emit_gate(c):
            cbT = scanenv["cbT"]
            pL = ps("pL", [128, 512], F32, c)
            pT = ps("pT", [128, 512], F32, c)
            pL2 = ps("pL2", [128, 512], F32, c)
            pT2 = ps("pT2", [128, 512], F32, c)
            gwb = sb("gwb", [128, 128], BF16, c)
            xslg = [sb(f"gxsl{i}", [128, 2048], BF16, c) for i in range(4)]
            lsb = sb("lsb", [8, 2048], F32, c)
            ltok = sb("ltok", [128, 8 * 16], F32, c)
            sc1 = sb("sc1", [128, 16], F32, c)
            es = sb("es", [128, 8 * 16], F32, c)
            cbtok = sb("cbtok", [128, 8 * 16], F32, c)
            P.dma("g", gwb[:], dr["gate"].ap(), reads=[dr["gate"]], writes=[gwb],
                  key="gate")
            regions = [(pL, 0), (pL2, 0), (pT, 0), (pT2, 0)]
            for kt in range(NKT):
                xsl_ = xslg[kt % 4]
                P.dma("y", xsl_[:], xn3_scr.ap()[kt * 128:(kt + 1) * 128, :],
                      reads=[xn3_scr], writes=[xsl_], key=f"gxrd{kt % 4}")
                for ch in range(4):
                    pr_, b0 = regions[ch]
                    P.op("t", lambda t, kt=kt, ch=ch, pr_=pr_, b0=b0, xsl_=xsl_: t.matmul(
                        pr_[b0:b0 + 8, 0:512], gwb[:, kt * 8:(kt + 1) * 8],
                        xsl_[:, ch * 512:(ch + 1) * 512],
                        start=(kt == 0), stop=(kt == NKT - 1)),
                        reads=[gwb, xsl_], writes=[pr_])
            for ch in range(4):
                pr_, b0 = regions[ch]
                P.op("s", lambda s, ch=ch, pr_=pr_, b0=b0: s.copy(
                    lsb[:, ch * 512:(ch + 1) * 512], pr_[b0:b0 + 8, 0:512]),
                    reads=[pr_], writes=[lsb])
            for tt in range(16):
                P.op("t", lambda t, tt=tt: t.transpose(
                    pT[0:128, 0:8], lsb[0:8, tt * 128:(tt + 1) * 128], idf[0:8, 0:8]),
                    reads=[lsb, idf], writes=[pT])
                P.op("v", lambda v, tt=tt: v.tensor_copy(ltok[:, tt * 8:(tt + 1) * 8],
                                                         pT[0:128, 0:8]),
                     reads=[pT], writes=[ltok])
            for tt in range(16):
                sl = ltok[:, tt * 8:(tt + 1) * 8]
                e_ = es[:, tt * 8:(tt + 1) * 8]
                cb = cbtok[:, tt * 8:(tt + 1) * 8]
                s1 = sc1[:, tt:tt + 1]
                P.op("v", lambda v, sl=sl, s1=s1: v.reduce_max(s1, sl, axis=AX.X,
                                                               negate=True),
                     reads=[ltok], writes=[sc1])
                P.op("s", lambda s, sl=sl, e_=e_, s1=s1: s.activation(
                    e_, sl, AF.Exp, bias=s1),
                    reads=[ltok, sc1], writes=[es])
                P.op("v", lambda v, e_=e_, s1=s1: v.reduce_sum(s1, e_, axis=AX.X),
                     reads=[es], writes=[sc1])
                P.op("v", lambda v, s1=s1: v.reciprocal(s1, s1),
                     reads=[sc1], writes=[sc1])
                P.op("v", lambda v, e_=e_, s1=s1: v.tensor_scalar(
                    e_, e_, s1, None, ALU.mult),
                    reads=[es, sc1], writes=[es])
                P.op("v", lambda v, e_=e_, s1=s1: v.reduce_max(s1, e_, axis=AX.X),
                     reads=[es], writes=[sc1])
                P.op("v", lambda v, e_=e_, cb=cb, s1=s1: v.tensor_scalar(
                    cb, e_, s1, None, ALU.is_equal),
                    reads=[es, sc1], writes=[cbtok])
                P.op("v", lambda v, e_=e_, cb=cb, sl=sl: v.scalar_tensor_tensor(
                    sl, cb, -1e30, e_, ALU.mult, ALU.add),
                    reads=[cbtok, es], writes=[ltok])
                P.op("v", lambda v, sl=sl, s1=s1: v.reduce_max(s1, sl, axis=AX.X),
                     reads=[ltok], writes=[sc1])
                P.op("v", lambda v, sl=sl, s1=s1: v.tensor_scalar(
                    sl, sl, s1, None, ALU.is_equal),
                    reads=[ltok, sc1], writes=[ltok])
                P.op("v", lambda v, sl=sl, cb=cb: v.tensor_tensor(cb, cb, sl, ALU.add),
                     reads=[cbtok, ltok], writes=[cbtok])
                P.op("v", lambda v, e_=e_, cb=cb: v.tensor_tensor(cb, cb, e_, ALU.mult),
                     reads=[cbtok, es], writes=[cbtok])
            for tt in range(16):
                P.op("t", lambda t, tt=tt: t.transpose(
                    pT[0:8, 0:128], cbtok[:, tt * 8:(tt + 1) * 8], idf[:]),
                    reads=[cbtok, idf], writes=[pT])
                P.op("v", lambda v, tt=tt: v.tensor_copy(cbT[:, tt * 128:(tt + 1) * 128],
                                                         pT[0:8, 0:128]),
                     reads=[pT], writes=[cbT])
            if dev:
                dump("cbtok", cbtok[:], [128, 128])
                dump("ltok", es[:], [128, 128])

        def emit_ffn(c):
            cbT = scanenv["cbT"]
            """dense MoE (10 uniform expert passes incl. 2 shared halves) + LN4 + residual.

            PSUM: 6 rotating [128x512] banks for gate/up/down + 2 stats banks.
            gate/up matmul groups are ordered so Silu/act evacuation pipelines
            behind the next group's matmuls.
            """
            pF = [ps(f"pF{i}", [128, 512], F32, c) for i in range(6)]
            pStB = ps("pStB2", [128, 512], F32, c)
            pStC = ps("pStC2", [128, 512], F32, c)
            pB = pStB
            xnh = sb("xnh", [128, NKT * 1024], BF16, c)
            gub = [sb(f"gub{i}", [128, 4096], BF16, c) for i in range(2)]
            dwb = sb("dwb", [128, NFKT * 2048], BF16, c)
            act = sb("act", [128, NFKT * 1024], BF16, c)
            ysb = sb("ysb", [128, NKT * 1024], F32, c)
            cbB = sb("cbB", [128, 1024], BF16, c)
            selb = sb("selb", [8, 1024], BF16, c)
            tbf = sb("tbf", [128, 1024], BF16, c)
            ybf = [sb(f"ybf{i}", [128, 512], BF16, c) for i in range(2)]
            ysq = [sb(f"ysq{i}", [128, 512], BF16, c) for i in range(2)]
            musb = sb("mu4", [128, 512], F32, c)
            rsb = sb("r4", [128, 512], F32, c)
            t1a = sb("t1f", [128, 512], F32, c)
            t1 = [t1a, t1a]
            t2 = [sb(f"t2f{i}", [128, 512], F32, c) for i in range(2)]
            x1sl = [sb(f"x1f{i}", [128, 512], F32, c) for i in range(2)]

            P.dma("g", selb[:], dr["selc"].ap(), reads=[dr["selc"]], writes=[selb],
                  key="selc")

            def half_loads(t0):
                for kt in range(NKT):
                    P.dma("y", xnh[:, kt * 1024:(kt + 1) * 1024],
                          xn3_scr.ap()[kt * 128:(kt + 1) * 128, t0:t0 + 1024],
                          reads=[xn3_scr], writes=[xnh], key="xnh")
                P.dma("g", gub[0][:], dr["gup"].ap()[0, 0], reads=[dr["gup"]],
                      writes=[gub[0]], key="gu0")

            half_loads(0)
            for hf in range(2):
                t0 = hf * 1024
                P.op("v", lambda v: v.memset(ysb[:], 0.0), writes=[ysb])
                blk = 0
                for e in range(10):
                    routed = e < 8
                    if routed:
                        for n2 in range(2):
                            P.op("t", lambda t, e=e, n2=n2, t0=t0: t.matmul(
                                pB[:], selb[:, e * 128:(e + 1) * 128],
                                cbT[0:8, t0 + n2 * 512:t0 + (n2 + 1) * 512],
                                start=True, stop=True),
                                reads=[selb, cbT], writes=[pB])
                            P.op("s", lambda s, n2=n2: s.copy(
                                cbB[:, n2 * 512:(n2 + 1) * 512], pB[:]),
                                reads=[pB], writes=[cbB])
                    for pr in range(NPAIR):
                        geb = gub[blk % 2]
                        ne, npr = (e, pr + 1) if pr + 1 < NPAIR else (e + 1, 0)
                        if ne < 10:
                            P.dma("g", gub[(blk + 1) % 2][:], dr["gup"].ap()[ne, npr],
                                  reads=[dr["gup"]], writes=[gub[(blk + 1) % 2]],
                                  key=f"gu{(blk + 1) % 2}")
                        if pr < NFKT:
                            P.dma("g", dwb[:, pr * 2048:(pr + 1) * 2048],
                                  dr["dwn"].ap()[e][:, pr * 2048:(pr + 1) * 2048],
                                  reads=[dr["dwn"]], writes=[dwb], key="dw0")
                        for gi, (col0, pfi) in enumerate(((0, 0), (512, 1))):
                            for kt in range(NKT):
                                P.op("t", lambda t, kt=kt, col0=col0, pfi=pfi, geb=geb: t.matmul(
                                    pF[pfi][:], geb[:, kt * 256:kt * 256 + 128],
                                    xnh[:, kt * 1024 + col0: kt * 1024 + col0 + 512],
                                    start=(kt == 0), stop=(kt == NKT - 1)),
                                    reads=[geb, xnh], writes=[pF[pfi]])
                            P.op("s", lambda s, col0=col0, pfi=pfi: s.activation(
                                tbf[:, col0:col0 + 512], pF[pfi][:], AF.Silu),
                                reads=[pF[pfi]], writes=[tbf])
                        for col0, pfi in ((0, 2), (512, 3)):
                            for kt in range(NKT):
                                P.op("t", lambda t, kt=kt, col0=col0, pfi=pfi, geb=geb: t.matmul(
                                    pF[pfi][:], geb[:, kt * 256 + 128:kt * 256 + 256],
                                    xnh[:, kt * 1024 + col0: kt * 1024 + col0 + 512],
                                    start=(kt == 0), stop=(kt == NKT - 1)),
                                    reads=[geb, xnh], writes=[pF[pfi]])
                            if routed:
                                P.op("v", lambda v, col0=col0: v.tensor_tensor(
                                    tbf[:, col0:col0 + 512], tbf[:, col0:col0 + 512],
                                    cbB[:, col0:col0 + 512], ALU.mult),
                                    reads=[tbf, cbB], writes=[tbf])
                            asl_c = (pr * 1024 + col0)
                            P.op("v", lambda v, asl_c=asl_c, col0=col0, pfi=pfi: v.tensor_tensor(
                                act[:, asl_c:asl_c + 512], tbf[:, col0:col0 + 512],
                                pF[pfi][:], ALU.mult),
                                reads=[tbf, pF[pfi]], writes=[act])
                        blk += 1
                    for mt in range(NKT):
                        pa = pF[(2 * mt) % 4]
                        pb_ = pF[(2 * mt + 1) % 4]
                        for fk in range(NFKT):
                            for n2, pd in ((0, pa), (1, pb_)):
                                P.op("t", lambda t, fk=fk, mt=mt, n2=n2, pd=pd: t.matmul(
                                    pd[:], dwb[:, fk * 2048 + mt * 128: fk * 2048 + (mt + 1) * 128],
                                    act[:, fk * 1024 + n2 * 512: fk * 1024 + (n2 + 1) * 512],
                                    start=(fk == 0), stop=(fk == NFKT - 1)),
                                    reads=[dwb, act], writes=[pd])
                        for n2, pd in ((0, pa), (1, pb_)):
                            ysl = ysb[:, mt * 1024 + n2 * 512: mt * 1024 + (n2 + 1) * 512]
                            P.op("v", lambda v, ysl=ysl, pd=pd: v.tensor_tensor(
                                ysl, ysl, pd[:], ALU.add),
                                reads=[ysb, pd], writes=[ysb])
                if hf == 0:
                    half_loads(1024)
                for ch in range(2):
                    c0 = ch * 512
                    for mt in range(NKT + 1):
                        if mt < NKT:
                            ysl = ysb[:, mt * 1024 + c0: mt * 1024 + c0 + 512]
                            P.op("g", lambda g, ysl=ysl, mt=mt: g.tensor_copy(
                                ybf[mt % 2][:], ysl),
                                reads=[ysb], writes=[ybf[mt % 2]])
                            P.op("s", lambda s, mt=mt: s.activation(
                                ysq[mt % 2][:], ybf[mt % 2][:], AF.Square),
                                reads=[ybf[mt % 2]], writes=[ysq[mt % 2]])
                        if mt > 0:
                            pm = mt - 1
                            P.op("t", lambda t, pm=pm: t.matmul(
                                pStB[:], ones128[:], ybf[pm % 2][:], start=(pm == 0),
                                stop=(pm == NKT - 1)),
                                reads=[ones128, ybf[pm % 2]], writes=[pStB])
                            P.op("t", lambda t, pm=pm: t.matmul(
                                pStC[:], ones128[:], ysq[pm % 2][:], start=(pm == 0),
                                stop=(pm == NKT - 1)),
                                reads=[ones128, ysq[pm % 2]], writes=[pStC])
                    P.op("v", lambda v: v.tensor_scalar(musb[:], pStB[:], 1.0 / H, None,
                                                        ALU.mult),
                         reads=[pStB], writes=[musb])
                    P.op("s", lambda s: s.activation(t1[0][:], musb[:], AF.Square),
                         reads=[musb], writes=[t1[0]])
                    P.op("v", lambda v: v.scalar_tensor_tensor(t1[0][:], pStC[:], 1.0 / H,
                                                               t1[0][:], ALU.mult,
                                                               ALU.subtract),
                         reads=[pStC, t1[0]], writes=[t1[0]])
                    P.op("v", lambda v: v.tensor_scalar(t1[0][:], t1[0][:], float(EPS), None, ALU.add),
                         reads=[t1[0]], writes=[t1[0]])
                    P.op("s", lambda s: s.activation(t1[0][:], t1[0][:], AF.Ln),
                         reads=[t1[0]], writes=[t1[0]])
                    P.op("s", lambda s: s.activation(rsb[:], t1[0][:], AF.Exp, scale=-0.5),
                         reads=[t1[0]], writes=[rsb])
                    for mt in range(NKT):
                        ysl = ysb[:, mt * 1024 + c0: mt * 1024 + c0 + 512]
                        ta, tb, xs = t1[mt % 2], t2[mt % 2], x1sl[mt % 2]
                        P.dma("y", xs[:], x1_scr.ap()[mt * 128:(mt + 1) * 128,
                                                      t0 + c0:t0 + c0 + 512],
                              reads=[x1_scr], writes=[xs], key=f"x1rd{mt % 2}")
                        P.op("v", lambda v, ysl=ysl, ta=ta: v.tensor_tensor(
                            ta[:], ysl, musb[:], ALU.subtract),
                            reads=[ysb, musb], writes=[ta])
                        P.op("g", lambda g, ta=ta, tb=tb: g.tensor_tensor(
                            tb[:], ta[:], rsb[:], ALU.mult),
                            reads=[ta, rsb], writes=[tb])
                        P.op("v", lambda v, xs=xs, tb=tb: v.tensor_tensor(
                            xs[:], xs[:], tb[:], ALU.add),
                            reads=[tb, xs], writes=[xs])
                        P.dma("s", out.ap()[mt * 128:(mt + 1) * 128,
                                            t0 + c0:t0 + c0 + 512],
                              xs[:], reads=[xs], writes=[out], key="outst")

        with ExitStack() as cA:
            xnb = sb("xnb", [128, NKT * 2048], BF16, cA)
            h0b = sb("h0b", [128, NH * 128], F32, cA)
            onesf = sb("onesf", [128, TC], BF16, cA)
            scanenv["xnb"] = xnb
            scanenv["h0b"] = h0b
            scanenv["onesf"] = onesf
            P.op("v", lambda v: v.memset(onesf[:], 1.0), writes=[onesf])
            with ExitStack() as c1:
                emit_ln_to_bf16("lp", "xPT", xnb, c1)
                if dev:
                    dump("xnp", xnb[:, 0:2048], [128, 2048], BF16)
            P.barrier()
            if upto_i >= 1:
                with ExitStack() as c2:
                    emit_scan_phase(False, c2)
                P.barrier()
            if upto_i >= 2:
                with ExitStack() as c3:
                    emit_ln_to_bf16("l1", "xT", xnb, c3)
                    if dev:
                        dump("xn", xnb[:, 0:2048], [128, 2048], BF16)
                P.barrier()
            if upto_i >= 3:
                with ExitStack() as c4:
                    emit_scan_phase(True, c4)
                P.barrier()
        if upto_i >= 4:
            with ExitStack() as c5:
                emit_p2(c5)
            P.barrier()
        if upto_i >= 5:
            with ExitStack() as cGF:
                scanenv["cbT"] = sb("cbT", [8, TC], BF16, cGF)
                with ExitStack() as c6:
                    emit_gate(c6)
                P.barrier()
                if upto_i >= 6:
                    with ExitStack() as c7:
                        emit_ffn(c7)
                    P.barrier()

        P.final_wait()
        block = ctx.enter_context(nc.Block())
        P.finalize_into(block)

    return nc



def _host_inputs(x, Wq, Wf, Wi, Wo, g_weight, gate_weight, gup, dwn, sgup, sdwn):
    def wgroups(W):
        return np.ascontiguousarray(
            W.reshape(16, 128, 16, 128).transpose(2, 1, 0, 3).reshape(16, 128, 2048))

    wq_h = wgroups(Wq)
    wf_h = wgroups(Wf)
    wi_h = wgroups(Wi)
    wo_h = np.ascontiguousarray(Wo.reshape(16, 128, 2048))
    gwv_h = np.ascontiguousarray(g_weight.reshape(16, 128).T)
    gate_h = np.ascontiguousarray(
        gate_weight.T.reshape(16, 128, 8).transpose(1, 0, 2).reshape(128, 128))

    def pair_block(G):
        o = np.empty((NPAIR, 128, 4096), np.float32)
        for p in range(NPAIR):
            blk = np.concatenate([G[p * 128:(p + 1) * 128, :],
                                  G[FEXP + p * 128:FEXP + (p + 1) * 128, :]], axis=0)
            o[p] = blk.T.reshape(16, 128, 256).transpose(1, 0, 2).reshape(128, 4096)
        return o

    gup_h = np.empty((10, NPAIR, 128, 4096), np.float32)
    for e in range(8):
        gup_h[e] = pair_block(gup[e])
    A = np.concatenate([sgup[0:FEXP, :], sgup[2 * FEXP:3 * FEXP, :]], axis=0)
    Bb = np.concatenate([sgup[FEXP:2 * FEXP, :], sgup[3 * FEXP:4 * FEXP, :]], axis=0)
    gup_h[8] = pair_block(A)
    gup_h[9] = pair_block(Bb)

    def down_block(DT):
        return np.ascontiguousarray(
            DT.reshape(NFKT, 128, 2048).transpose(1, 0, 2).reshape(128, NFKT * 2048))

    dwn_h = np.empty((10, 128, NFKT * 2048), np.float32)
    for e in range(8):
        dwn_h[e] = down_block(dwn[e].T)
    sdT = sdwn.T
    dwn_h[8] = down_block(sdT[0:FEXP])
    dwn_h[9] = down_block(sdT[FEXP:2 * FEXP])

    ident = np.eye(128, dtype=np.float32)
    tri = np.triu(np.ones((64, 64), np.float32))
    smask_h = np.concatenate([tri, tri], axis=0)
    selc = np.zeros((8, 1024), np.float32)
    for e in range(8):
        selc[e, e * 128:(e + 1) * 128] = 1.0

    shared = dict(wq=wq_h, wf=wf_h, wi=wi_h, wo=wo_h, gwv=gwv_h, gate=gate_h,
                  gup=gup_h, dwn=dwn_h, ident=ident, smask=smask_h, selc=selc)
    in_maps = []
    zeros = np.zeros((H, TC), np.float32)
    for core in range(8):
        b, hf = core // 2, core % 2
        m = dict(shared)
        m["xT"] = np.ascontiguousarray(x[b, hf * TC:(hf + 1) * TC, :].T)
        m["xPT"] = (np.ascontiguousarray(x[b, 0:TC, :].T) if hf == 1 else zeros)
        in_maps.append(m)
    return in_maps


def kernel(L0_attn_x, L0_attn_Wq, L0_attn_Wf, L0_attn_Wi, L0_attn_Wo,
           L0_attn_g_weight, L0_ffn_gate_weight, L0_ffn_gate_up_weights,
           L0_ffn_down_weights, L0_ffn_shared_gate_up, L0_ffn_shared_down):
    in_maps = _host_inputs(
        np.asarray(L0_attn_x, np.float32), np.asarray(L0_attn_Wq, np.float32),
        np.asarray(L0_attn_Wf, np.float32), np.asarray(L0_attn_Wi, np.float32),
        np.asarray(L0_attn_Wo, np.float32), np.asarray(L0_attn_g_weight, np.float32),
        np.asarray(L0_ffn_gate_weight, np.float32),
        np.asarray(L0_ffn_gate_up_weights, np.float32),
        np.asarray(L0_ffn_down_weights, np.float32),
        np.asarray(L0_ffn_shared_gate_up, np.float32),
        np.asarray(L0_ffn_shared_down, np.float32))
    nc = build_core_program()
    res = run_bass_kernel_spmd(nc, in_maps, core_ids=list(range(8)))
    outp = np.empty((Bsz, S, H), np.float32)
    for core in range(8):
        b, hf = core // 2, core % 2
        outp[b, hf * TC:(hf + 1) * TC, :] = np.asarray(res.results[core]["outT"]).T
    return outp



# version 16
# speedup vs baseline: 1.0059x; 1.0059x over previous
import numpy as np
import ml_dtypes
from contextlib import ExitStack

import concourse.bass as bass
import concourse.mybir as mybir
from concourse.bass_utils import run_bass_kernel_spmd

F32 = mybir.dt.float32
BF16 = mybir.dt.bfloat16
AF = mybir.ActivationFunctionType
ALU = mybir.AluOpType
AX = mybir.AxisListType
BF = ml_dtypes.bfloat16

Bsz, S, H = 4, 4096, 2048
NH, DK, DV = 16, 128, 128
TC = 2048
NKT = H // 128
EPS = 1e-5
CP0 = 128
CP1 = 64
NCH0 = TC // CP0
NCH1 = TC // CP1
FEXP = 1408
NPAIR = 11
NFKT = 11

ENGINES = ("t", "v", "s", "g", "y")


class _Buf:
    __slots__ = ("name", "writer", "readers")

    def __init__(self, name):
        self.name = name
        self.writer = None
        self.readers = []


class Prog:

    def __init__(self, nc, ctx):
        self.nc = nc
        self.ctx = ctx
        self.sem = {e: ctx.enter_context(nc.semaphore(f"S{e}")) for e in ENGINES}
        self.cnt = {e: 0 for e in ENGINES}
        self.dsem = {}
        self.dcnt = {}
        self.waited = {e: {} for e in ENGINES}
        self.ops = {e: [] for e in ENGINES}
        self.bufs = {}
        self.pending = {e: [] for e in ENGINES}
        self.defer = None
        self.dev_outs = []

    def buf(self, h):
        key = id(h)
        b = self.bufs.get(key)
        if b is None:
            b = _Buf(getattr(h, "name", str(key)))
            self.bufs[key] = b
        return b

    def _collect(self, eng, reads, writes):
        want = {}

        def add(t):
            if t is None:
                return
            sk, val, peng = t
            if peng == "t" and eng == "t":
                return
            if want.get(sk, 0) < val:
                want[sk] = val

        rb = [self.buf(h) for h in reads]
        wb = [self.buf(h) for h in writes]
        for b in rb:
            add(b.writer)
        for b in wb:
            add(b.writer)
            for t in b.readers:
                add(t)
        waits = []
        weng = self.waited[eng]
        for sk, val in want.items():
            if weng.get(sk, 0) < val:
                waits.append((sk, val))
                weng[sk] = val
        return waits, rb, wb

    def op(self, eng, fn, reads=(), writes=()):
        if self.defer is not None:
            self.defer.append((eng, fn, tuple(reads), tuple(writes)))
            return None
        waits, rb, wb = self._collect(eng, reads, writes)
        waits = self.pending[eng] + waits
        self.pending[eng] = []
        self.cnt[eng] += 1
        ticket = (eng, self.cnt[eng], eng)
        for b in rb:
            b.readers.append(ticket)
        for b in wb:
            b.writer = ticket
            b.readers = []
        self.ops[eng].append((waits, fn, self.sem[eng], 1))
        return ticket

    def emit_now(self, item):
        eng, fn, reads, writes = item
        waits, rb, wb = self._collect(eng, reads, writes)
        waits = self.pending[eng] + waits
        self.pending[eng] = []
        self.cnt[eng] += 1
        ticket = (eng, self.cnt[eng], eng)
        for b in rb:
            b.readers.append(ticket)
        for b in wb:
            b.writer = ticket
            b.readers = []
        self.ops[eng].append((waits, fn, self.sem[eng], 1))
        return ticket

    def dma(self, q, out, in_, reads=(), writes=(), key=None):
        assert key is not None
        if key not in self.dsem:
            self.dsem[key] = self.ctx.enter_context(self.nc.semaphore(f"D{key}"))
            self.dcnt[key] = 0
        waits, rb, wb = self._collect(q, reads, writes)
        waits = self.pending[q] + waits
        self.pending[q] = []
        self.dcnt[key] += 16
        ticket = (f"D{key}", self.dcnt[key], "dma")
        for b in rb:
            b.readers.append(ticket)
        for b in wb:
            b.writer = ticket
            b.readers = []

        def fn(eng, out=out, in_=in_):
            return eng.dma_start(out=out, in_=in_)

        self.ops[q].append((waits, fn, self.dsem[key], 16))
        return ticket

    def barrier(self):
        waits = []
        for e in ENGINES:
            if self.cnt[e]:
                waits.append((e, self.cnt[e]))
        for k, v in self.dcnt.items():
            waits.append((f"D{k}", v))
        for e in ENGINES:
            weng = self.waited[e]
            pend = []
            for sk, val in waits:
                if weng.get(sk, 0) < val:
                    pend.append((sk, val))
                    weng[sk] = val
            self.pending[e] = self.pending[e] + pend
        self.bufs = {}

    def _sem_obj(self, sk):
        if isinstance(sk, str) and len(sk) > 1 and sk.startswith("D"):
            return self.dsem[sk[1:]]
        return self.sem[sk]

    def finalize_into(self, block):
        P = self

        def runner(eng_name):
            def run(engobj):
                for waits, fn, inc_sem, inc_amt in P.ops[eng_name]:
                    for sk, val in waits:
                        engobj.wait_ge(P._sem_obj(sk), val)
                    fn(engobj).then_inc(inc_sem, inc_amt)
            return run

        block.sync(runner("y"))
        block.tensor(runner("t"))
        block.vector(runner("v"))
        block.scalar(runner("s"))
        block.gpsimd(runner("g"))

    def final_wait(self):
        self.barrier()
        waits = self.pending["y"]
        self.pending["y"] = []
        self.ops["y"].append((waits, lambda eng: eng.nop(), self.sem["y"], 1))



PHASES = ("ln1p", "p0", "ln1", "p1", "p2", "gate", "ffn")


def build_core_program(upto="ffn", dev=False):
    nc = bass.Bass("TRN2", target_bir_lowering=False)
    upto_i = PHASES.index(upto)
    scratch_kind = "ExternalOutput" if dev else "Internal"

    dr = {}

    def din(name, shape, dt):
        dr[name] = nc.dram_tensor(name, shape, dt, kind="ExternalInput")
        return dr[name]

    def dscr(name, shape, dt):
        dr[name] = nc.dram_tensor(name, shape, dt, kind=scratch_kind)
        return dr[name]

    din("xT", [H, TC], F32)
    din("xPT", [H, TC], F32)
    din("wq", [16, 128, 2048], F32)
    din("wf", [16, 128, 2048], F32)
    din("wi", [16, 128, 2048], F32)
    din("wo", [16, 128, 2048], F32)
    din("gwv", [128, 16], F32)
    din("gate", [128, 128], F32)
    din("gup", [10, NPAIR, 128, 4096], F32)
    din("dwn", [10, 128, NFKT * 2048], F32)
    din("ident", [128, 128], F32)
    din("smask", [128, 64], F32)
    din("selc", [8, 1024], F32)

    o_scr = dscr("o_scr", [H, TC], BF16)
    ao_scr = dscr("ao_scr", [H, TC], F32)
    x1_scr = dscr("x1_scr", [H, TC], F32)
    xn3_scr = dscr("xn3_scr", [H, TC], BF16)
    out = nc.dram_tensor("outT", [H, TC], F32, kind="ExternalOutput")
    dr["outT"] = out

    with ExitStack() as ctx:
        P = Prog(nc, ctx)

        _uid = [0]

        def sb(name, shape, dt, c=ctx):
            _uid[0] += 1
            return c.enter_context(nc.sbuf_tensor(f"{name}_{_uid[0]}", shape, dt))

        def ps(name, shape, dt, c=ctx):
            _uid[0] += 1
            return c.enter_context(nc.psum_tensor(f"{name}_{_uid[0]}", shape, dt))

        def dump(name, src_sb, shape, dt=F32):
            if not dev:
                return
            d = nc.dram_tensor(f"dbg_{name}", shape, dt, kind="ExternalOutput")
            P.dma("y", d.ap(), src_sb, reads=[src_sb.tensor if hasattr(src_sb, "tensor") else src_sb],
                  key=f"dbg_{name}")

        ones128 = sb("ones128", [128, 128], BF16)
        idb = sb("idb", [128, 128], BF16)
        idf = sb("idf", [128, 128], F32)
        smk = sb("smk", [128, 64], BF16)
        gwvb = sb("gwvb", [128, 16], F32)

        P.dma("g", idb[:], dr["ident"].ap(), writes=[idb], key="consts")
        P.dma("y", idf[:], dr["ident"].ap(), writes=[idf], key="c2")
        P.dma("g", smk[:], dr["smask"].ap(), writes=[smk], key="consts")
        P.dma("y", gwvb[:], dr["gwv"].ap(), writes=[gwvb], key="c2")
        P.op("v", lambda v: v.memset(ones128[:], 1.0), writes=[ones128])
        scanenv = {}

        def emit_ln_to_bf16(tag, src_name, dst_sb, c):
            src_dram = dr[src_name]
            pSum = ps(f"pSum_{tag}", [128, 2048], F32, c)
            pSq = ps(f"pSq_{tag}", [128, 2048], F32, c)
            xsl = [sb(f"xsl_{tag}{i}", [128, 2048], F32, c) for i in range(2)]
            xbsl = [sb(f"xb_{tag}{i}", [128, 2048], BF16, c) for i in range(2)]
            xsq = [sb(f"xq_{tag}{i}", [128, 2048], BF16, c) for i in range(2)]
            musb = sb(f"mu_{tag}", [128, 2048], F32, c)
            rsb = sb(f"r_{tag}", [128, 2048], F32, c)
            tmp = sb(f"tmp_{tag}", [128, 2048], F32, c)
            t1 = [sb(f"t1_{tag}{i}", [128, 512], F32, c) for i in range(2)]

            for kt in range(NKT):
                sl, xb, xq = xsl[kt % 2], xbsl[kt % 2], xsq[kt % 2]
                P.dma("y", sl[:], src_dram.ap()[kt * 128:(kt + 1) * 128, :],
                      reads=[src_dram], writes=[sl], key=f"x{tag}{kt % 2}")
                P.op("g", lambda g, sl=sl, xb=xb: g.tensor_copy(xb[:], sl[:]),
                     reads=[sl], writes=[xb])
                P.op("s", lambda s, xb=xb, xq=xq: s.activation(xq[:], xb[:], AF.Square),
                     reads=[xb], writes=[xq])
                for ch in range(4):
                    c0 = ch * 512
                    P.op("t", lambda t, kt=kt, c0=c0, xb=xb: t.matmul(
                        pSum[:, c0:c0 + 512], ones128[:], xb[:, c0:c0 + 512],
                        start=(kt == 0), stop=(kt == NKT - 1)),
                        reads=[ones128, xb], writes=[pSum])
                    P.op("t", lambda t, kt=kt, c0=c0, xq=xq: t.matmul(
                        pSq[:, c0:c0 + 512], ones128[:], xq[:, c0:c0 + 512],
                        start=(kt == 0), stop=(kt == NKT - 1)),
                        reads=[ones128, xq], writes=[pSq])
            P.op("v", lambda v: v.tensor_scalar(musb[:], pSum[:], 1.0 / H, None, ALU.mult),
                 reads=[pSum], writes=[musb])
            P.op("s", lambda s: s.activation(tmp[:], musb[:], AF.Square),
                 reads=[musb], writes=[tmp])
            P.op("v", lambda v: v.scalar_tensor_tensor(tmp[:], pSq[:], 1.0 / H, tmp[:],
                                                       ALU.mult, ALU.subtract),
                 reads=[pSq, tmp], writes=[tmp])
            P.op("v", lambda v: v.tensor_scalar(tmp[:], tmp[:], float(EPS), None, ALU.add),
                 reads=[tmp], writes=[tmp])
            P.op("s", lambda s: s.activation(tmp[:], tmp[:], AF.Ln),
                 reads=[tmp], writes=[tmp])
            P.op("s", lambda s: s.activation(rsb[:], tmp[:], AF.Exp, scale=-0.5),
                 reads=[tmp], writes=[rsb])
            for kt in range(NKT):
                sl = xsl[kt % 2]
                P.dma("y", sl[:], src_dram.ap()[kt * 128:(kt + 1) * 128, :],
                      reads=[src_dram], writes=[sl], key=f"x{tag}{kt % 2}")
                for ch in range(4):
                    c0 = ch * 512
                    tb = t1[ch % 2]
                    P.op("v", lambda v, sl=sl, c0=c0, tb=tb: v.tensor_tensor(
                        tb[:], sl[:, c0:c0 + 512], musb[:, c0:c0 + 512], ALU.subtract),
                        reads=[sl, musb], writes=[tb])
                    P.op("g", lambda g, kt=kt, c0=c0, tb=tb: g.tensor_tensor(
                        dst_sb[:, kt * 2048 + c0: kt * 2048 + c0 + 512],
                        tb[:], rsb[:, c0:c0 + 512], ALU.mult),
                        reads=[tb, rsb], writes=[dst_sb])

        def emit_scan_phase(main, c):
            xnb = scanenv["xnb"]
            h0b = scanenv["h0b"]
            onesf = scanenv["onesf"]
            C = CP0 if not main else CP1
            NCH = NCH0 if not main else NCH1
            NPC = NCH // 2 if main else NCH

            pPJ = [ps(f"pPJ{i}", [128, 512], F32, c) for i in range(2)]
            pU = [ps(f"pU{i}", [128, 512], F32, c) for i in range(2)]
            pTr = [ps(f"pTr{i}", [128, 1024], BF16, c) for i in range(2)]
            if main:
                pS = [ps(f"pS{i}", [128, 512], F32, c) for i in range(2)]
                pO = pS[0]

            wnames = ("wq", "wf", "wi") if main else ("wf", "wi")
            wbuf = {n: [sb(f"wb_{n}{i}", [128, 2048], BF16, c) for i in range(2)]
                    for n in wnames}

            fsb = [sb(f"fsb{i}", [128, 2048], F32, c) for i in range(2)]
            gsb = [sb(f"gsb{i}", [128, 2048], F32, c) for i in range(2)]
            esb1 = sb("esb", [128, 2048], F32, c)
            esb = [esb1, esb1]
            tmps = [sb(f"tmps{i}", [128, 2048], F32, c) for i in range(2)]
            ksb = sb("ksb", [128, 2048], BF16, c)
            vsb = [sb(f"vsb{i}", [128, 2048], BF16, c) for i in range(2)]
            khat = [sb(f"khat{i}", [128, 2048], BF16, c) for i in range(2)]
            eD = [sb(f"eD{i}", [128, 32], F32, c) for i in range(2)]
            Bc = sb("Bc", [128, 32], F32, c)
            Bp = sb("Bp", [128, 32], F32, c)
            vt_all = sb("vt_all", [128, 2048], BF16, c)
            ovtags = [[object() for _ in range(5)] for _ in range(2)]
            hB = sb("hB", [128, 128], F32, c)
            if main:
                qb = [sb(f"qb{i}", [128, 2048], BF16, c) for i in range(2)]
                qtl = [sb(f"qtl{i}", [128, 2048], BF16, c) for i in range(2)]
                Hst = sb("Hst", [128, NCH1 * 128], BF16, c)

            def load_w(n, h):
                P.dma("g", wbuf[n][h % 2][:], dr[n].ap()[h],
                      reads=[dr[n]], writes=[wbuf[n][h % 2]], key=f"w{n}{h % 2}")

            for n in wnames:
                load_w(n, 0)
                load_w(n, 1)

            def proj(wn, h, evac_dst):
                wb = wbuf[wn][h % 2]
                for grp in range(4):
                    pt = pPJ[grp % 2]
                    t0 = grp * 512
                    for kt in range(NKT):
                        P.op("t", lambda t, kt=kt, t0=t0, wb=wb, pt=pt: t.matmul(
                            pt[:], wb[:, kt * 128:(kt + 1) * 128],
                            xnb[:, kt * 2048 + t0: kt * 2048 + t0 + 512],
                            start=(kt == 0), stop=(kt == NKT - 1)),
                            reads=[wb, xnb], writes=[pt])
                    if grp % 2 == 0:
                        P.op("s", lambda s, pt=pt, t0=t0: s.copy(
                            evac_dst[:, t0:t0 + 512], pt[:]),
                            reads=[pt], writes=[evac_dst])
                    else:
                        P.op("v", lambda v, pt=pt, t0=t0: v.tensor_copy(
                            evac_dst[:, t0:t0 + 512], pt[:]),
                            reads=[pt], writes=[evac_dst])
                if h + 2 < NH:
                    load_w(wn, h + 2)

            def logsig(dst, src, hb, extra_writes=()):
                tp, ep = tmps[hb], esb[hb]
                P.op("v", lambda v: v.scalar_tensor_tensor(tp[:], src[:], -1.0, src[:],
                                                           ALU.mult, ALU.max),
                     reads=[src], writes=[tp] + list(extra_writes))
                P.op("s", lambda s: s.activation(ep[:], tp[:], AF.Exp, scale=-1.0),
                     reads=[tp], writes=[ep])
                P.op("s", lambda s: s.activation(ep[:], ep[:], AF.Ln, bias=1.0),
                     reads=[ep], writes=[ep])
                P.op("v", lambda v: v.scalar_tensor_tensor(dst[:], src[:], 0.0, ep[:],
                                                           ALU.min, ALU.subtract),
                     reads=[src, ep], writes=[dst])

            def emit_proj_elem(h, proj_now=True):
                hb = h % 2
                fs, gs, ep, tp = fsb[hb], gsb[hb], esb[hb], tmps[hb]
                Aw = fs
                subc = gs
                if proj_now:
                    proj("wf", h, fs)
                    proj("wi", h, vsb[hb])
                    if main:
                        proj("wq", h, qb[hb])
                    return
                P.op("s", lambda s: s.activation(ep[:], fs[:], AF.Sigmoid),
                     reads=[fs], writes=([ep] + list(ovtags[hb])))
                if main:
                    P.op("s", lambda s, hb=hb: s.activation(qb[hb][:], qb[hb][:],
                                                            AF.Silu),
                         reads=[qb[hb]], writes=[qb[hb]])
                P.op("s", lambda s: s.activation(gs[:], ep[:], AF.Ln),
                     reads=[ep], writes=[gs])
                P.op("v", lambda v: v.tensor_scalar(ksb[:], ep[:], -1.0, 1.0,
                                                    ALU.mult, ALU.add),
                     reads=[ep], writes=[ksb])
                P.op("v", lambda v: v.tensor_tensor_scan(Aw[:], onesf[:], gs[:], 0.0,
                                                         ALU.mult, ALU.add),
                     reads=[onesf, gs], writes=[Aw])
                P.op("v", lambda v: v.tensor_copy(Bc[:, 0:NCH], Aw[:, C - 1::C]),
                     reads=[Aw], writes=[Bc])
                P.op("v", lambda v: v.memset(Bp[:, 0:1], 0.0), writes=[Bp])
                P.op("v", lambda v: v.tensor_copy(Bp[:, 1:NCH], Bc[:, 0:NCH - 1]),
                     reads=[Bc], writes=[Bp])
                P.op("v", lambda v, hb=hb: v.tensor_tensor(eD[hb][:, 0:NCH], Bc[:, 0:NCH],
                                                           Bp[:, 0:NCH], ALU.subtract),
                     reads=[Bc, Bp], writes=[eD[hb]])
                P.op("s", lambda s, hb=hb: s.activation(eD[hb][:, 0:NCH], eD[hb][:, 0:NCH],
                                                        AF.Exp),
                     reads=[eD[hb]], writes=[eD[hb]])
                if not main:
                    for cch in range(NCH):
                        P.op("v", lambda v, cch=cch: v.tensor_scalar(
                            tp[:, cch * C:(cch + 1) * C], Aw[:, cch * C:(cch + 1) * C],
                            Bc[:, cch:cch + 1], None, ALU.subtract),
                            reads=[Aw, Bc], writes=[tp])
                    P.op("s", lambda s: s.activation(ep[:], tp[:], AF.Exp, scale=-1.0),
                         reads=[tp], writes=[ep])
                    P.op("v", lambda v, hb=hb: v.tensor_tensor(khat[hb][:], ksb[:], ep[:],
                                                               ALU.mult),
                         reads=[ksb, ep], writes=[khat[hb]])
                else:
                    for cch in range(NCH):
                        P.op("v", lambda v, cch=cch: v.tensor_scalar(
                            subc[:, cch * C:(cch + 1) * C], Aw[:, cch * C:(cch + 1) * C],
                            Bp[:, cch:cch + 1], None, ALU.subtract),
                            reads=[Aw, Bp], writes=[subc])
                    P.op("s", lambda s: s.activation(tp[:], subc[:], AF.Exp),
                         reads=[subc], writes=[tp])
                    P.op("v", lambda v, hb=hb: v.scalar_tensor_tensor(
                        qtl[hb][:], qb[hb][:], float(DK ** -0.5), tp[:],
                        ALU.mult, ALU.mult),
                        reads=[qb[hb], tp], writes=[qtl[hb]])
                    P.op("v", lambda v: v.tensor_scalar(subc[:], subc[:], -80.0, None,
                                                        ALU.max),
                         reads=[subc], writes=[subc])
                    P.op("s", lambda s: s.activation(ep[:], subc[:], AF.Exp, scale=-1.0),
                         reads=[subc], writes=[ep])
                    P.op("v", lambda v, hb=hb: v.tensor_tensor(khat[hb][:], ksb[:],
                                                               ep[:], ALU.mult),
                         reads=[ksb, ep], writes=[khat[hb]])

            def emit_ktl(h):
                hb = h % 2
                tbits = tmps[hb][:].bitcast(BF16)
                ktlv = tbits[:, 2048:4096]
                tagKL = ovtags[hb][4]
                for cch in range(NCH):
                    P.op("v", lambda v, cch=cch, hb=hb, ktlv=ktlv: v.tensor_scalar(
                        ktlv[:, cch * C:(cch + 1) * C],
                        khat[hb][:, cch * C:(cch + 1) * C],
                        eD[hb][:, cch:cch + 1], None, ALU.mult),
                        reads=[khat[hb], eD[hb]], writes=[tagKL])

            def emit_scan_part(h, pump=lambda: None):
                hb = h % 2
                hsl = h0b[:, h * 128:(h + 1) * 128]
                tpb = tmps[hb]
                tbits = tpb[:].bitcast(BF16)
                hupd = tpb[:, 0:128]
                ktv = [tbits[:, 256:384], tbits[:, 384:512]]
                ostg = tbits[:, 512:1024]
                sm_all = tbits[:, 1024:2048] if main else None
                ktlv = tbits[:, 2048:4096]
                tagK, tagS, tagO, tagH, tagKL = ovtags[hb]
                if main:
                    P.op("g", lambda g, hsl=hsl: g.tensor_copy(Hst[:, 0:128], hsl),
                         reads=[h0b], writes=[Hst])
                else:
                    P.op("v", lambda v, hsl=hsl: v.memset(hsl, 0.0), writes=[h0b])
                def emit_pair_transposes(p):
                    c0 = p * 128
                    pt = pTr[p % 2]
                    P.op("t", lambda t, c0=c0, pt=pt, hb=hb: t.transpose(
                        pt[:, 0:128], vsb[hb][:, c0:c0 + 128], idb[:]),
                        reads=[vsb[hb], idb], writes=[pt])
                    ktsrc = ktlv[:, c0:c0 + 128] if main else khat[hb][:, c0:c0 + 128]
                    ktr = tagKL if main else khat[hb]
                    P.op("t", lambda t, pt=pt, ktsrc=ktsrc: t.transpose(
                        pt[:, 128:256], ktsrc, idb[:]),
                        reads=[ktr, idb], writes=[pt])
                    P.op("s", lambda s, c0=c0, pt=pt: s.copy(
                        vt_all[:, c0:c0 + 128], pt[:, 0:128]),
                        reads=[pt], writes=[vt_all])
                    P.op("s", lambda s, p=p, pt=pt: s.copy(
                        ktv[p % 2], pt[:, 128:256]),
                        reads=[pt], writes=[tagK])

                def emit_pair_su(p):
                    if main:
                        psx = pS[p % 2]
                        for sc_ in range(2):
                            cch = p * 2 + sc_
                            cc0 = cch * C
                            b0 = sc_ * 64
                            P.op("t", lambda t, cc0=cc0, b0=b0, psx=psx, hb=hb: t.matmul(
                                psx[b0:b0 + 64, 0:64], khat[hb][:, cc0:cc0 + 64],
                                qtl[hb][:, cc0:cc0 + 64], start=True, stop=True),
                                reads=[khat[hb], qtl[hb]], writes=[psx])
                        P.op("v", lambda v, p=p, psx=psx: v.tensor_tensor(
                            sm_all[:, p * 64:(p + 1) * 64], psx[:, 0:64],
                            smk[:], ALU.mult),
                            reads=[psx, smk], writes=[tagS])
                    if main:
                        for sc_ in range(2):
                            cch = p * 2 + sc_
                            pux = pU[cch % 2]
                            b0 = sc_ * 64
                            P.op("t", lambda t, p=p, b0=b0, pux=pux: t.matmul(
                                pux[:, 0:128],
                                ktv[p % 2][b0:b0 + 64, :],
                                vt_all[b0:b0 + 64, p * 128:(p + 1) * 128],
                                start=True, stop=True),
                                reads=[tagK, vt_all], writes=[pux])
                            hsrc = hsl if cch % 2 == 0 else hB[:]
                            hdst = hB[:] if cch % 2 == 0 else hsl
                            rsrc = h0b if cch % 2 == 0 else hB
                            rdst = hB if cch % 2 == 0 else h0b
                            P.op("v", lambda v, hsrc=hsrc, hdst=hdst, cch=cch, hb=hb, pux=pux:
                                 v.scalar_tensor_tensor(hdst, hsrc,
                                                        eD[hb][:, cch:cch + 1],
                                                        pux[:, 0:128], ALU.mult, ALU.add),
                                 reads=[rsrc, eD[hb], pux], writes=[rdst])
                            if cch + 1 < NCH:
                                P.op("g", lambda g, hdst=hdst, cch=cch: g.tensor_copy(
                                    Hst[:, (cch + 1) * 128:(cch + 2) * 128], hdst),
                                    reads=[rdst], writes=[Hst])
                    else:
                        cch = p
                        pux = pU[cch % 2]
                        P.op("t", lambda t, p=p, pux=pux: t.matmul(
                            pux[:, 0:128], ktv[p % 2],
                            vt_all[:, p * 128:(p + 1) * 128],
                            start=True, stop=True),
                            reads=[tagK, vt_all], writes=[pux])
                        P.op("v", lambda v, hsl=hsl, cch=cch, pux=pux, hb=hb: v.scalar_tensor_tensor(
                            hsl, hsl, eD[hb][:, cch:cch + 1], pux[:, 0:128],
                            ALU.mult, ALU.add),
                            reads=[h0b, eD[hb], pux], writes=[h0b])

                emit_pair_transposes(0)
                for p in range(NPC):
                    if p + 1 < NPC:
                        emit_pair_transposes(p + 1)
                    emit_pair_su(p)
                    pump()
                if main:
                    for cch in range(NCH):
                        cc0 = cch * C
                        oslot = (cch % 8) * 64
                        pr = cch // 2
                        b0 = (cch % 2) * 64
                        P.op("t", lambda t, cch=cch, cc0=cc0, oslot=oslot, hb=hb: t.matmul(
                            pO[:, oslot:oslot + 64], Hst[:, cch * 128:(cch + 1) * 128],
                            qtl[hb][:, cc0:cc0 + 64], start=True, stop=False),
                            reads=[Hst, qtl[hb]], writes=[pO])
                        P.op("t", lambda t, b0=b0, pr=pr, oslot=oslot: t.matmul(
                            pO[:, oslot:oslot + 64], vt_all[b0:b0 + 64, pr * 128:(pr + 1) * 128],
                            sm_all[b0:b0 + 64, pr * 64:(pr + 1) * 64],
                            start=False, stop=True),
                            reads=[vt_all, tagS], writes=[pO])
                        if cch % 8 == 7:
                            e0 = (cch // 8) * 512
                            P.op("s", lambda s: s.copy(ostg, pO[:]),
                                 reads=[pO], writes=[tagO])
                            P.dma("s", o_scr.ap()[h * 128:(h + 1) * 128, e0:e0 + 512],
                                  ostg, reads=[tagO], writes=[o_scr], key="ost")

            from collections import deque

            def build_chain(h):
                P.defer = []
                emit_proj_elem(h, proj_now=False)
                if main:
                    emit_ktl(h)
                lst = P.defer
                P.defer = None
                return deque(lst)

            def flush(q, k=None):
                n = len(q) if k is None else min(k, len(q))
                for _ in range(n):
                    P.emit_now(q.popleft())

            emit_proj_elem(0, proj_now=True)
            q0 = build_chain(0)
            flush(q0)
            emit_proj_elem(1, proj_now=True)
            chainq = build_chain(1)
            flush(chainq, 6)
            for h in range(NH):
                emit_scan_part(h, lambda: flush(chainq, 8))
                flush(chainq)
                if h + 2 < NH:
                    emit_proj_elem(h + 2, proj_now=True)
                    chainq = build_chain(h + 2)
            if dev:
                dump("h0b" + ("m" if main else "p"), h0b[:], [128, NH * 128])

        def emit_p2(c):
            pStA = ps("pStA", [128, 2048], F32, c)
            pW = [ps(f"pW{i}", [128, 512], F32, c) for i in range(2)]
            pStB = ps("pStB", [128, 512], F32, c)
            pStC = ps("pStC", [128, 512], F32, c)
            osq1 = sb("osq", [128, 2048], BF16, c)
            osq2 = [osq1, osq1]
            obf_all = sb("obf_all", [128, NKT * 2048], BF16, c)
            wob = sb("wob", [128, NKT * 2048], BF16, c)
            r2 = sb("r2", [128, 2048], F32, c)
            tmp = sb("tmp2", [128, 2048], F32, c)
            musb = sb("mu2", [128, 512], F32, c)
            rsb = sb("rln2", [128, 512], F32, c)
            mu3 = sb("mu3", [128, 512], F32, c)
            r3 = sb("r3", [128, 512], F32, c)
            aosl = [sb(f"aosl{i}", [128, 512], F32, c) for i in range(2)]
            wsl = [sb(f"wsl{i}", [128, 512], F32, c) for i in range(2)]
            aob = [sb(f"aob{i}", [128, 512], BF16, c) for i in range(2)]
            asq = [sb(f"asq{i}", [128, 512], BF16, c) for i in range(2)]
            x1sl = [sb(f"x1sl{i}", [128, 512], F32, c) for i in range(2)]
            x1b_all = sb("x1b_all", [128, NKT * 512], BF16, c)
            xsq = [sb(f"xsq{i}", [128, 512], BF16, c) for i in range(2)]
            t1l = [sb(f"t1b{i}", [128, 512], F32, c) for i in range(2)]
            t1 = t1l[0]
            t2 = [sb(f"t2b{i}", [128, 512], F32, c) for i in range(2)]
            xnb3 = [sb(f"xnb3{i}", [128, 512], BF16, c) for i in range(2)]
            mu2p = sb("mu2p", [128, 512], F32, c)
            rs2p = sb("rs2p", [128, 512], F32, c)

            wotag = [object() for _ in range(4)]
            obtag = [object() for _ in range(4)]
            for kt in range(NKT):
                P.dma("g", wob[:, kt * 2048:(kt + 1) * 2048], dr["wo"].ap()[kt],
                      reads=[dr["wo"]], writes=[wotag[kt % 4]], key=f"wo{kt % 4}")
                P.op("g", lambda g, kt=kt: g.tensor_scalar(
                    wob[:, kt * 2048:(kt + 1) * 2048],
                    wob[:, kt * 2048:(kt + 1) * 2048],
                    gwvb[:, kt:kt + 1], None, ALU.mult),
                    reads=[wotag[kt % 4], gwvb], writes=[wotag[kt % 4]])
            for kt in range(NKT):
                osl_ = obf_all[:, kt * 2048:(kt + 1) * 2048]
                oq = osq2[kt % 2]
                P.dma("y", osl_, o_scr.ap()[kt * 128:(kt + 1) * 128, :],
                      reads=[o_scr], writes=[obtag[kt % 4]], key=f"ord{kt % 4}")
                P.op("s", lambda s, osl_=osl_, oq=oq, kt=kt: s.activation(oq[:], osl_, AF.Square),
                     reads=[obtag[kt % 4]], writes=[oq])
                for ch in range(4):
                    c0 = ch * 512
                    P.op("t", lambda t, kt=kt, c0=c0, oq=oq: t.matmul(
                        pStA[:, c0:c0 + 512], ones128[:], oq[:, c0:c0 + 512],
                        start=(kt == 0), stop=(kt == NKT - 1)),
                        reads=[ones128, oq], writes=[pStA])
            P.op("v", lambda v: v.tensor_scalar(tmp[:], pStA[:], 1.0 / H, None, ALU.mult),
                 reads=[pStA], writes=[tmp])
            P.op("v", lambda v: v.tensor_scalar(tmp[:], tmp[:], float(EPS), None, ALU.add),
                 reads=[tmp], writes=[tmp])
            P.op("s", lambda s: s.activation(tmp[:], tmp[:], AF.Ln),
                 reads=[tmp], writes=[tmp])
            P.op("s", lambda s: s.activation(r2[:], tmp[:], AF.Exp, scale=-0.5),
                 reads=[tmp], writes=[r2])

            def stats_to_mur(mu_dst, r_dst, rpre=None, srcB=None, srcC=None,
                             srcT=()):
                sB = pStB[:] if srcB is None else srcB
                sC = pStC[:] if srcC is None else srcC
                rT = list(srcT) if srcT else ([pStB, pStC] if srcB is None else [])
                P.op("v", lambda v: v.tensor_scalar(mu_dst[:], sB, 1.0 / H,
                                                    None, ALU.mult),
                     reads=[rT[0]], writes=[mu_dst])
                if rpre is not None:
                    P.op("v", lambda v: v.tensor_tensor(mu_dst[:], mu_dst[:], rpre,
                                                        ALU.mult),
                         reads=[mu_dst], writes=[mu_dst])
                P.op("s", lambda s: s.activation(t1[:], mu_dst[:], AF.Square),
                     reads=[mu_dst], writes=[t1])
                P.op("v", lambda v: v.scalar_tensor_tensor(t1[:], sC, 1.0 / H,
                                                           t1[:], ALU.mult,
                                                           ALU.subtract)
                     if rpre is None else
                     v.tensor_scalar(t2[0][:], sC, 1.0 / H, None, ALU.mult),
                     reads=[rT[-1], t1], writes=[t1] if rpre is None else [t2[0]])
                if rpre is not None:
                    P.op("v", lambda v: v.tensor_tensor(t2[0][:], t2[0][:], rpre, ALU.mult),
                         reads=[t2[0]], writes=[t2[0]])
                    P.op("v", lambda v: v.tensor_tensor(t2[0][:], t2[0][:], rpre, ALU.mult),
                         reads=[t2[0]], writes=[t2[0]])
                    P.op("v", lambda v: v.tensor_tensor(t1[:], t2[0][:], t1[:],
                                                        ALU.subtract),
                         reads=[t1, t2[0]], writes=[t1])
                P.op("v", lambda v: v.tensor_scalar(t1[:], t1[:], float(EPS), None,
                                                    ALU.add),
                     reads=[t1], writes=[t1])
                P.op("s", lambda s: s.activation(t1[:], t1[:], AF.Ln),
                     reads=[t1], writes=[t1])
                P.op("s", lambda s: s.activation(r_dst[:], t1[:], AF.Exp, scale=-0.5),
                     reads=[t1], writes=[r_dst])

            def wo_part(ch):
                c0 = ch * 512
                for mt in range(NKT + 1):
                    if mt < NKT:
                        pm = pW[mt % 2]
                        for kt in range(NKT):
                            P.op("t", lambda t, pm=pm, mt=mt, kt=kt, c0=c0: t.matmul(
                                pm[:], wob[:, kt * 2048 + mt * 128: kt * 2048 + (mt + 1) * 128],
                                obf_all[:, kt * 2048 + c0: kt * 2048 + c0 + 512],
                                start=(kt == 0), stop=(kt == NKT - 1)),
                                reads=[wotag[kt % 4], obtag[kt % 4]], writes=[pm])
                        asl = wsl[mt % 2]
                        P.op("s", lambda s, pm=pm, asl=asl: s.copy(asl[:], pm[:]),
                             reads=[pm], writes=[asl])
                        P.dma("s", ao_scr.ap()[mt * 128:(mt + 1) * 128, c0:c0 + 512],
                              asl[:], reads=[asl], writes=[ao_scr], key="aost")
                        P.op("g", lambda g, mt=mt, asl=asl: g.tensor_copy(
                            aob[mt % 2][:], asl[:]),
                            reads=[asl], writes=[aob[mt % 2]])
                        P.op("s", lambda s, mt=mt: s.activation(
                            asq[mt % 2][:], aob[mt % 2][:], AF.Square),
                            reads=[aob[mt % 2]], writes=[asq[mt % 2]])
                    if mt > 0:
                        q = mt - 1
                        P.op("t", lambda t, q=q: t.matmul(
                            pStB[:], ones128[:], aob[q % 2][:], start=(q == 0),
                            stop=(q == NKT - 1)),
                            reads=[ones128, aob[q % 2]], writes=[pStB])
                        P.op("t", lambda t, q=q: t.matmul(
                            pStC[:], ones128[:], asq[q % 2][:], start=(q == 0),
                            stop=(q == NKT - 1)),
                            reads=[ones128, asq[q % 2]], writes=[pStC])

            def x1_part(ch):
                c0 = ch * 512
                r2c = r2[:, c0:c0 + 512]
                P.op("v", lambda v, r2c=r2c: v.reciprocal(t2[0][:], r2c),
                     reads=[r2], writes=[t2[0]])
                P.op("v", lambda v: v.tensor_tensor(mu2p[:], musb[:], t2[0][:],
                                                    ALU.mult),
                     reads=[musb, t2[0]], writes=[mu2p])
                P.op("v", lambda v, r2c=r2c: v.tensor_tensor(rs2p[:], rsb[:], r2c,
                                                             ALU.mult),
                     reads=[rsb, r2], writes=[rs2p])
                for mt in range(NKT + 1):
                    if mt < NKT:
                        asl, xsl_, tb = aosl[mt % 2], x1sl[mt % 2], t2[mt % 2]
                        xb = x1b_all[:, mt * 512:(mt + 1) * 512]
                        P.dma("y", asl[:], ao_scr.ap()[mt * 128:(mt + 1) * 128,
                                                       c0:c0 + 512],
                              reads=[ao_scr], writes=[asl], key=f"aord{mt % 2}")
                        P.dma("y", xsl_[:], dr["xT"].ap()[mt * 128:(mt + 1) * 128,
                                                          c0:c0 + 512],
                              reads=[dr["xT"]], writes=[xsl_], key=f"xrd{mt % 2}")
                        ta = t1l[mt % 2]
                        P.op("v", lambda v, asl=asl, ta=ta: v.tensor_tensor(
                            ta[:], asl[:], mu2p[:], ALU.subtract),
                            reads=[asl, mu2p], writes=[ta])
                        P.op("g", lambda g, tb=tb, ta=ta: g.tensor_tensor(tb[:], ta[:],
                                                                          rs2p[:],
                                                                          ALU.mult),
                             reads=[ta, rs2p], writes=[tb])
                        P.op("v", lambda v, xsl_=xsl_, tb=tb: v.tensor_tensor(
                            xsl_[:], xsl_[:], tb[:], ALU.add),
                            reads=[tb, xsl_], writes=[xsl_])
                        P.dma("s", x1_scr.ap()[mt * 128:(mt + 1) * 128, c0:c0 + 512],
                              xsl_[:], reads=[xsl_], writes=[x1_scr], key="x1st")
                        P.op("g", lambda g, xb=xb, xsl_=xsl_: g.tensor_copy(xb, xsl_[:]),
                             reads=[xsl_], writes=[x1b_all])
                        P.op("s", lambda s, mt=mt, xb=xb: s.activation(
                            xsq[mt % 2][:], xb, AF.Square),
                            reads=[x1b_all], writes=[xsq[mt % 2]])
                    if mt > 0:
                        q = mt - 1
                        xbq = x1b_all[:, q * 512:(q + 1) * 512]
                        P.op("t", lambda t, q=q, xbq=xbq: t.matmul(
                            pStA[:, 0:512], ones128[:], xbq, start=(q == 0),
                            stop=(q == NKT - 1)),
                            reads=[ones128, x1b_all], writes=[pStA])
                        P.op("t", lambda t, q=q: t.matmul(
                            pStA[:, 512:1024], ones128[:], xsq[q % 2][:], start=(q == 0),
                            stop=(q == NKT - 1)),
                            reads=[ones128, xsq[q % 2]], writes=[pStA])
                stats_to_mur(mu3, r3, srcB=pStA[:, 0:512], srcC=pStA[:, 512:1024],
                             srcT=[pStA, pStA])
                for mt in range(NKT):
                    xb = x1b_all[:, mt * 512:(mt + 1) * 512]
                    ab = xnb3[mt % 2]
                    ta = t1l[mt % 2]
                    P.op("v", lambda v, xb=xb, ta=ta: v.tensor_tensor(ta[:], xb, mu3[:],
                                                               ALU.subtract),
                         reads=[x1b_all, mu3], writes=[ta])
                    P.op("g", lambda g, ab=ab, ta=ta: g.tensor_tensor(ab[:], ta[:],
                                                               r3[:], ALU.mult),
                         reads=[ta, r3], writes=[ab])
                    P.dma("s", xn3_scr.ap()[mt * 128:(mt + 1) * 128, c0:c0 + 512],
                          ab[:], reads=[ab], writes=[xn3_scr], key="xn3st")

            for ch in range(5):
                if ch >= 1:
                    stats_to_mur(musb, rsb, rpre=r2[:, (ch - 1) * 512:ch * 512])
                if ch < 4:
                    wo_part(ch)
                if ch >= 1:
                    x1_part(ch - 1)

        def emit_gate(c):
            cbT = scanenv["cbT"]
            pL = ps("pL", [128, 512], F32, c)
            pT = ps("pT", [128, 512], F32, c)
            pL2 = ps("pL2", [128, 512], F32, c)
            pT2 = ps("pT2", [128, 512], F32, c)
            gwb = sb("gwb", [128, 128], BF16, c)
            xslg = [sb(f"gxsl{i}", [128, 2048], BF16, c) for i in range(4)]
            lsb = sb("lsb", [8, 2048], F32, c)
            ltok = sb("ltok", [128, 8 * 16], F32, c)
            sc1 = sb("sc1", [128, 16], F32, c)
            es = sb("es", [128, 8 * 16], F32, c)
            cbtok = sb("cbtok", [128, 8 * 16], F32, c)
            P.dma("g", gwb[:], dr["gate"].ap(), reads=[dr["gate"]], writes=[gwb],
                  key="gate")
            regions = [(pL, 0), (pL2, 0), (pT, 0), (pT2, 0)]
            for kt in range(NKT):
                xsl_ = xslg[kt % 4]
                P.dma("y", xsl_[:], xn3_scr.ap()[kt * 128:(kt + 1) * 128, :],
                      reads=[xn3_scr], writes=[xsl_], key=f"gxrd{kt % 4}")
                for ch in range(4):
                    pr_, b0 = regions[ch]
                    P.op("t", lambda t, kt=kt, ch=ch, pr_=pr_, b0=b0, xsl_=xsl_: t.matmul(
                        pr_[b0:b0 + 8, 0:512], gwb[:, kt * 8:(kt + 1) * 8],
                        xsl_[:, ch * 512:(ch + 1) * 512],
                        start=(kt == 0), stop=(kt == NKT - 1)),
                        reads=[gwb, xsl_], writes=[pr_])
            for ch in range(4):
                pr_, b0 = regions[ch]
                P.op("s", lambda s, ch=ch, pr_=pr_, b0=b0: s.copy(
                    lsb[:, ch * 512:(ch + 1) * 512], pr_[b0:b0 + 8, 0:512]),
                    reads=[pr_], writes=[lsb])
            for tt in range(16):
                P.op("t", lambda t, tt=tt: t.transpose(
                    pT[0:128, 0:8], lsb[0:8, tt * 128:(tt + 1) * 128], idf[0:8, 0:8]),
                    reads=[lsb, idf], writes=[pT])
                P.op("v", lambda v, tt=tt: v.tensor_copy(ltok[:, tt * 8:(tt + 1) * 8],
                                                         pT[0:128, 0:8]),
                     reads=[pT], writes=[ltok])
            for tt in range(16):
                sl = ltok[:, tt * 8:(tt + 1) * 8]
                e_ = es[:, tt * 8:(tt + 1) * 8]
                cb = cbtok[:, tt * 8:(tt + 1) * 8]
                s1 = sc1[:, tt:tt + 1]
                P.op("v", lambda v, sl=sl, s1=s1: v.reduce_max(s1, sl, axis=AX.X,
                                                               negate=True),
                     reads=[ltok], writes=[sc1])
                P.op("s", lambda s, sl=sl, e_=e_, s1=s1: s.activation(
                    e_, sl, AF.Exp, bias=s1),
                    reads=[ltok, sc1], writes=[es])
                P.op("v", lambda v, e_=e_, s1=s1: v.reduce_sum(s1, e_, axis=AX.X),
                     reads=[es], writes=[sc1])
                P.op("v", lambda v, s1=s1: v.reciprocal(s1, s1),
                     reads=[sc1], writes=[sc1])
                P.op("v", lambda v, e_=e_, s1=s1: v.tensor_scalar(
                    e_, e_, s1, None, ALU.mult),
                    reads=[es, sc1], writes=[es])
                P.op("v", lambda v, e_=e_, s1=s1: v.reduce_max(s1, e_, axis=AX.X),
                     reads=[es], writes=[sc1])
                P.op("v", lambda v, e_=e_, cb=cb, s1=s1: v.tensor_scalar(
                    cb, e_, s1, None, ALU.is_equal),
                    reads=[es, sc1], writes=[cbtok])
                P.op("v", lambda v, e_=e_, cb=cb, sl=sl: v.scalar_tensor_tensor(
                    sl, cb, -1e30, e_, ALU.mult, ALU.add),
                    reads=[cbtok, es], writes=[ltok])
                P.op("v", lambda v, sl=sl, s1=s1: v.reduce_max(s1, sl, axis=AX.X),
                     reads=[ltok], writes=[sc1])
                P.op("v", lambda v, sl=sl, s1=s1: v.tensor_scalar(
                    sl, sl, s1, None, ALU.is_equal),
                    reads=[ltok, sc1], writes=[ltok])
                P.op("v", lambda v, sl=sl, cb=cb: v.tensor_tensor(cb, cb, sl, ALU.add),
                     reads=[cbtok, ltok], writes=[cbtok])
                P.op("v", lambda v, e_=e_, cb=cb: v.tensor_tensor(cb, cb, e_, ALU.mult),
                     reads=[cbtok, es], writes=[cbtok])
            for tt in range(16):
                P.op("t", lambda t, tt=tt: t.transpose(
                    pT[0:8, 0:128], cbtok[:, tt * 8:(tt + 1) * 8], idf[:]),
                    reads=[cbtok, idf], writes=[pT])
                P.op("v", lambda v, tt=tt: v.tensor_copy(cbT[:, tt * 128:(tt + 1) * 128],
                                                         pT[0:8, 0:128]),
                     reads=[pT], writes=[cbT])
            if dev:
                dump("cbtok", cbtok[:], [128, 128])
                dump("ltok", es[:], [128, 128])

        def emit_ffn(c):
            cbT = scanenv["cbT"]
            """dense MoE (10 uniform expert passes incl. 2 shared halves) + LN4 + residual.

            PSUM: 6 rotating [128x512] banks for gate/up/down + 2 stats banks.
            gate/up matmul groups are ordered so Silu/act evacuation pipelines
            behind the next group's matmuls.
            """
            pF = [ps(f"pF{i}", [128, 512], F32, c) for i in range(6)]
            pStB = ps("pStB2", [128, 512], F32, c)
            pStC = ps("pStC2", [128, 512], F32, c)
            pB = pStB
            xnh = sb("xnh", [128, NKT * 1024], BF16, c)
            gub = [sb(f"gub{i}", [128, 4096], BF16, c) for i in range(2)]
            dwb = sb("dwb", [128, NFKT * 2048], BF16, c)
            act = sb("act", [128, NFKT * 1024], BF16, c)
            ysb = sb("ysb", [128, NKT * 1024], F32, c)
            cbB = sb("cbB", [128, 1024], BF16, c)
            selb = sb("selb", [8, 1024], BF16, c)
            tbf = sb("tbf", [128, 1024], BF16, c)
            ybf = [sb(f"ybf{i}", [128, 512], BF16, c) for i in range(2)]
            ysq = [sb(f"ysq{i}", [128, 512], BF16, c) for i in range(2)]
            musb = sb("mu4", [128, 512], F32, c)
            rsb = sb("r4", [128, 512], F32, c)
            t1a = sb("t1f", [128, 512], F32, c)
            t1 = [t1a, t1a]
            t2 = [sb(f"t2f{i}", [128, 512], F32, c) for i in range(2)]
            x1sl = [sb(f"x1f{i}", [128, 512], F32, c) for i in range(2)]

            P.dma("g", selb[:], dr["selc"].ap(), reads=[dr["selc"]], writes=[selb],
                  key="selc")

            def half_loads(t0):
                for kt in range(NKT):
                    P.dma("y", xnh[:, kt * 1024:(kt + 1) * 1024],
                          xn3_scr.ap()[kt * 128:(kt + 1) * 128, t0:t0 + 1024],
                          reads=[xn3_scr], writes=[xnh], key="xnh")
                P.dma("g", gub[0][:], dr["gup"].ap()[0, 0], reads=[dr["gup"]],
                      writes=[gub[0]], key="gu0")

            half_loads(0)
            for hf in range(2):
                t0 = hf * 1024
                P.op("v", lambda v: v.memset(ysb[:], 0.0), writes=[ysb])
                blk = 0
                for e in range(10):
                    routed = e < 8
                    if routed:
                        for n2 in range(2):
                            P.op("t", lambda t, e=e, n2=n2, t0=t0: t.matmul(
                                pB[:], selb[:, e * 128:(e + 1) * 128],
                                cbT[0:8, t0 + n2 * 512:t0 + (n2 + 1) * 512],
                                start=True, stop=True),
                                reads=[selb, cbT], writes=[pB])
                            P.op("s", lambda s, n2=n2: s.copy(
                                cbB[:, n2 * 512:(n2 + 1) * 512], pB[:]),
                                reads=[pB], writes=[cbB])
                    for pr in range(NPAIR):
                        geb = gub[blk % 2]
                        ne, npr = (e, pr + 1) if pr + 1 < NPAIR else (e + 1, 0)
                        if ne < 10:
                            P.dma("g", gub[(blk + 1) % 2][:], dr["gup"].ap()[ne, npr],
                                  reads=[dr["gup"]], writes=[gub[(blk + 1) % 2]],
                                  key=f"gu{(blk + 1) % 2}")
                        if pr < NFKT:
                            P.dma("g", dwb[:, pr * 2048:(pr + 1) * 2048],
                                  dr["dwn"].ap()[e][:, pr * 2048:(pr + 1) * 2048],
                                  reads=[dr["dwn"]], writes=[dwb], key="dw0")
                        for gi, (col0, pfi) in enumerate(((0, 0), (512, 1))):
                            for kt in range(NKT):
                                P.op("t", lambda t, kt=kt, col0=col0, pfi=pfi, geb=geb: t.matmul(
                                    pF[pfi][:], geb[:, kt * 256:kt * 256 + 128],
                                    xnh[:, kt * 1024 + col0: kt * 1024 + col0 + 512],
                                    start=(kt == 0), stop=(kt == NKT - 1)),
                                    reads=[geb, xnh], writes=[pF[pfi]])
                            P.op("s", lambda s, col0=col0, pfi=pfi: s.activation(
                                tbf[:, col0:col0 + 512], pF[pfi][:], AF.Silu),
                                reads=[pF[pfi]], writes=[tbf])
                        for col0, pfi in ((0, 2), (512, 3)):
                            for kt in range(NKT):
                                P.op("t", lambda t, kt=kt, col0=col0, pfi=pfi, geb=geb: t.matmul(
                                    pF[pfi][:], geb[:, kt * 256 + 128:kt * 256 + 256],
                                    xnh[:, kt * 1024 + col0: kt * 1024 + col0 + 512],
                                    start=(kt == 0), stop=(kt == NKT - 1)),
                                    reads=[geb, xnh], writes=[pF[pfi]])
                            if routed:
                                P.op("v", lambda v, col0=col0: v.tensor_tensor(
                                    tbf[:, col0:col0 + 512], tbf[:, col0:col0 + 512],
                                    cbB[:, col0:col0 + 512], ALU.mult),
                                    reads=[tbf, cbB], writes=[tbf])
                            asl_c = (pr * 1024 + col0)
                            P.op("v", lambda v, asl_c=asl_c, col0=col0, pfi=pfi: v.tensor_tensor(
                                act[:, asl_c:asl_c + 512], tbf[:, col0:col0 + 512],
                                pF[pfi][:], ALU.mult),
                                reads=[tbf, pF[pfi]], writes=[act])
                        blk += 1
                    for mt in range(NKT):
                        pa = pF[(2 * mt) % 4]
                        pb_ = pF[(2 * mt + 1) % 4]
                        for fk in range(NFKT):
                            for n2, pd in ((0, pa), (1, pb_)):
                                P.op("t", lambda t, fk=fk, mt=mt, n2=n2, pd=pd: t.matmul(
                                    pd[:], dwb[:, fk * 2048 + mt * 128: fk * 2048 + (mt + 1) * 128],
                                    act[:, fk * 1024 + n2 * 512: fk * 1024 + (n2 + 1) * 512],
                                    start=(fk == 0), stop=(fk == NFKT - 1)),
                                    reads=[dwb, act], writes=[pd])
                        for n2, pd in ((0, pa), (1, pb_)):
                            ysl = ysb[:, mt * 1024 + n2 * 512: mt * 1024 + (n2 + 1) * 512]
                            P.op("v", lambda v, ysl=ysl, pd=pd: v.tensor_tensor(
                                ysl, ysl, pd[:], ALU.add),
                                reads=[ysb, pd], writes=[ysb])
                if hf == 0:
                    half_loads(1024)
                for ch in range(2):
                    c0 = ch * 512
                    for mt in range(NKT + 1):
                        if mt < NKT:
                            ysl = ysb[:, mt * 1024 + c0: mt * 1024 + c0 + 512]
                            P.op("g", lambda g, ysl=ysl, mt=mt: g.tensor_copy(
                                ybf[mt % 2][:], ysl),
                                reads=[ysb], writes=[ybf[mt % 2]])
                            P.op("s", lambda s, mt=mt: s.activation(
                                ysq[mt % 2][:], ybf[mt % 2][:], AF.Square),
                                reads=[ybf[mt % 2]], writes=[ysq[mt % 2]])
                        if mt > 0:
                            pm = mt - 1
                            P.op("t", lambda t, pm=pm: t.matmul(
                                pStB[:], ones128[:], ybf[pm % 2][:], start=(pm == 0),
                                stop=(pm == NKT - 1)),
                                reads=[ones128, ybf[pm % 2]], writes=[pStB])
                            P.op("t", lambda t, pm=pm: t.matmul(
                                pStC[:], ones128[:], ysq[pm % 2][:], start=(pm == 0),
                                stop=(pm == NKT - 1)),
                                reads=[ones128, ysq[pm % 2]], writes=[pStC])
                    P.op("v", lambda v: v.tensor_scalar(musb[:], pStB[:], 1.0 / H, None,
                                                        ALU.mult),
                         reads=[pStB], writes=[musb])
                    P.op("s", lambda s: s.activation(t1[0][:], musb[:], AF.Square),
                         reads=[musb], writes=[t1[0]])
                    P.op("v", lambda v: v.scalar_tensor_tensor(t1[0][:], pStC[:], 1.0 / H,
                                                               t1[0][:], ALU.mult,
                                                               ALU.subtract),
                         reads=[pStC, t1[0]], writes=[t1[0]])
                    P.op("v", lambda v: v.tensor_scalar(t1[0][:], t1[0][:], float(EPS), None, ALU.add),
                         reads=[t1[0]], writes=[t1[0]])
                    P.op("s", lambda s: s.activation(t1[0][:], t1[0][:], AF.Ln),
                         reads=[t1[0]], writes=[t1[0]])
                    P.op("s", lambda s: s.activation(rsb[:], t1[0][:], AF.Exp, scale=-0.5),
                         reads=[t1[0]], writes=[rsb])
                    for mt in range(NKT):
                        ysl = ysb[:, mt * 1024 + c0: mt * 1024 + c0 + 512]
                        ta, tb, xs = t1[mt % 2], t2[mt % 2], x1sl[mt % 2]
                        P.dma("y", xs[:], x1_scr.ap()[mt * 128:(mt + 1) * 128,
                                                      t0 + c0:t0 + c0 + 512],
                              reads=[x1_scr], writes=[xs], key=f"x1rd{mt % 2}")
                        P.op("v", lambda v, ysl=ysl, ta=ta: v.tensor_tensor(
                            ta[:], ysl, musb[:], ALU.subtract),
                            reads=[ysb, musb], writes=[ta])
                        P.op("g", lambda g, ta=ta, tb=tb: g.tensor_tensor(
                            tb[:], ta[:], rsb[:], ALU.mult),
                            reads=[ta, rsb], writes=[tb])
                        P.op("v", lambda v, xs=xs, tb=tb: v.tensor_tensor(
                            xs[:], xs[:], tb[:], ALU.add),
                            reads=[tb, xs], writes=[xs])
                        P.dma("s", out.ap()[mt * 128:(mt + 1) * 128,
                                            t0 + c0:t0 + c0 + 512],
                              xs[:], reads=[xs], writes=[out], key="outst")

        with ExitStack() as cA:
            xnb = sb("xnb", [128, NKT * 2048], BF16, cA)
            h0b = sb("h0b", [128, NH * 128], F32, cA)
            onesf = sb("onesf", [128, TC], BF16, cA)
            scanenv["xnb"] = xnb
            scanenv["h0b"] = h0b
            scanenv["onesf"] = onesf
            P.op("v", lambda v: v.memset(onesf[:], 1.0), writes=[onesf])
            with ExitStack() as c1:
                emit_ln_to_bf16("lp", "xPT", xnb, c1)
                if dev:
                    dump("xnp", xnb[:, 0:2048], [128, 2048], BF16)
            P.barrier()
            if upto_i >= 1:
                with ExitStack() as c2:
                    emit_scan_phase(False, c2)
                P.barrier()
            if upto_i >= 2:
                with ExitStack() as c3:
                    emit_ln_to_bf16("l1", "xT", xnb, c3)
                    if dev:
                        dump("xn", xnb[:, 0:2048], [128, 2048], BF16)
                P.barrier()
            if upto_i >= 3:
                with ExitStack() as c4:
                    emit_scan_phase(True, c4)
                P.barrier()
        if upto_i >= 4:
            with ExitStack() as c5:
                emit_p2(c5)
            P.barrier()
        if upto_i >= 5:
            with ExitStack() as cGF:
                scanenv["cbT"] = sb("cbT", [8, TC], BF16, cGF)
                with ExitStack() as c6:
                    emit_gate(c6)
                P.barrier()
                if upto_i >= 6:
                    with ExitStack() as c7:
                        emit_ffn(c7)
                    P.barrier()

        P.final_wait()
        block = ctx.enter_context(nc.Block())
        P.finalize_into(block)

    return nc



def _host_inputs(x, Wq, Wf, Wi, Wo, g_weight, gate_weight, gup, dwn, sgup, sdwn):
    def wgroups(W):
        return np.ascontiguousarray(
            W.reshape(16, 128, 16, 128).transpose(2, 1, 0, 3).reshape(16, 128, 2048))

    wq_h = wgroups(Wq)
    wf_h = wgroups(Wf)
    wi_h = wgroups(Wi)
    wo_h = np.ascontiguousarray(Wo.reshape(16, 128, 2048))
    gwv_h = np.ascontiguousarray(g_weight.reshape(16, 128).T)
    gate_h = np.ascontiguousarray(
        gate_weight.T.reshape(16, 128, 8).transpose(1, 0, 2).reshape(128, 128))

    def pair_block(G):
        o = np.empty((NPAIR, 128, 4096), np.float32)
        for p in range(NPAIR):
            blk = np.concatenate([G[p * 128:(p + 1) * 128, :],
                                  G[FEXP + p * 128:FEXP + (p + 1) * 128, :]], axis=0)
            o[p] = blk.T.reshape(16, 128, 256).transpose(1, 0, 2).reshape(128, 4096)
        return o

    gup_h = np.empty((10, NPAIR, 128, 4096), np.float32)
    for e in range(8):
        gup_h[e] = pair_block(gup[e])
    A = np.concatenate([sgup[0:FEXP, :], sgup[2 * FEXP:3 * FEXP, :]], axis=0)
    Bb = np.concatenate([sgup[FEXP:2 * FEXP, :], sgup[3 * FEXP:4 * FEXP, :]], axis=0)
    gup_h[8] = pair_block(A)
    gup_h[9] = pair_block(Bb)

    def down_block(DT):
        return np.ascontiguousarray(
            DT.reshape(NFKT, 128, 2048).transpose(1, 0, 2).reshape(128, NFKT * 2048))

    dwn_h = np.empty((10, 128, NFKT * 2048), np.float32)
    for e in range(8):
        dwn_h[e] = down_block(dwn[e].T)
    sdT = sdwn.T
    dwn_h[8] = down_block(sdT[0:FEXP])
    dwn_h[9] = down_block(sdT[FEXP:2 * FEXP])

    ident = np.eye(128, dtype=np.float32)
    tri = np.triu(np.ones((64, 64), np.float32))
    smask_h = np.concatenate([tri, tri], axis=0)
    selc = np.zeros((8, 1024), np.float32)
    for e in range(8):
        selc[e, e * 128:(e + 1) * 128] = 1.0

    shared = dict(wq=wq_h, wf=wf_h, wi=wi_h, wo=wo_h, gwv=gwv_h, gate=gate_h,
                  gup=gup_h, dwn=dwn_h, ident=ident, smask=smask_h, selc=selc)
    in_maps = []
    zeros = np.zeros((H, TC), np.float32)
    for core in range(8):
        b, hf = core // 2, core % 2
        m = dict(shared)
        m["xT"] = np.ascontiguousarray(x[b, hf * TC:(hf + 1) * TC, :].T)
        m["xPT"] = (np.ascontiguousarray(x[b, 0:TC, :].T) if hf == 1 else zeros)
        in_maps.append(m)
    return in_maps


def kernel(L0_attn_x, L0_attn_Wq, L0_attn_Wf, L0_attn_Wi, L0_attn_Wo,
           L0_attn_g_weight, L0_ffn_gate_weight, L0_ffn_gate_up_weights,
           L0_ffn_down_weights, L0_ffn_shared_gate_up, L0_ffn_shared_down):
    in_maps = _host_inputs(
        np.asarray(L0_attn_x, np.float32), np.asarray(L0_attn_Wq, np.float32),
        np.asarray(L0_attn_Wf, np.float32), np.asarray(L0_attn_Wi, np.float32),
        np.asarray(L0_attn_Wo, np.float32), np.asarray(L0_attn_g_weight, np.float32),
        np.asarray(L0_ffn_gate_weight, np.float32),
        np.asarray(L0_ffn_gate_up_weights, np.float32),
        np.asarray(L0_ffn_down_weights, np.float32),
        np.asarray(L0_ffn_shared_gate_up, np.float32),
        np.asarray(L0_ffn_shared_down, np.float32))
    nc = build_core_program()
    res = run_bass_kernel_spmd(nc, in_maps, core_ids=list(range(8)))
    outp = np.empty((Bsz, S, H), np.float32)
    for core in range(8):
        b, hf = core // 2, core % 2
        outp[b, hf * TC:(hf + 1) * TC, :] = np.asarray(res.results[core]["outT"]).T
    return outp



# version 17
# speedup vs baseline: 1.0922x; 1.0857x over previous
import numpy as np
import ml_dtypes
from contextlib import ExitStack

import concourse.bass as bass
import concourse.mybir as mybir
from concourse.bass_utils import run_bass_kernel_spmd

F32 = mybir.dt.float32
BF16 = mybir.dt.bfloat16
AF = mybir.ActivationFunctionType
ALU = mybir.AluOpType
AX = mybir.AxisListType
BF = ml_dtypes.bfloat16

Bsz, S, H = 4, 4096, 2048
NH, DK, DV = 16, 128, 128
TC = 2048
NKT = H // 128
EPS = 1e-5
CP0 = 128
CP1 = 64
NCH0 = TC // CP0
NCH1 = TC // CP1
FEXP = 1408
NPAIR = 11
NFKT = 11

ENGINES = ("t", "v", "s", "g", "y")


class _Buf:
    __slots__ = ("name", "writer", "readers")

    def __init__(self, name):
        self.name = name
        self.writer = None
        self.readers = []


class Prog:

    def __init__(self, nc, ctx):
        self.nc = nc
        self.ctx = ctx
        self.sem = {e: ctx.enter_context(nc.semaphore(f"S{e}")) for e in ENGINES}
        self.cnt = {e: 0 for e in ENGINES}
        self.dsem = {}
        self.dcnt = {}
        self.waited = {e: {} for e in ENGINES}
        self.ops = {e: [] for e in ENGINES}
        self.bufs = {}
        self.pending = {e: [] for e in ENGINES}
        self.defer = None
        self.dev_outs = []

    def buf(self, h):
        key = id(h)
        b = self.bufs.get(key)
        if b is None:
            b = _Buf(getattr(h, "name", str(key)))
            self.bufs[key] = b
        return b

    def _collect(self, eng, reads, writes):
        want = {}

        def add(t):
            if t is None:
                return
            sk, val, peng = t
            if peng == "t" and eng == "t":
                return
            if want.get(sk, 0) < val:
                want[sk] = val

        rb = [self.buf(h) for h in reads]
        wb = [self.buf(h) for h in writes]
        for b in rb:
            add(b.writer)
        for b in wb:
            add(b.writer)
            for t in b.readers:
                add(t)
        waits = []
        weng = self.waited[eng]
        for sk, val in want.items():
            if weng.get(sk, 0) < val:
                waits.append((sk, val))
                weng[sk] = val
        return waits, rb, wb

    def op(self, eng, fn, reads=(), writes=()):
        if self.defer is not None:
            self.defer.append((eng, fn, tuple(reads), tuple(writes)))
            return None
        waits, rb, wb = self._collect(eng, reads, writes)
        waits = self.pending[eng] + waits
        self.pending[eng] = []
        self.cnt[eng] += 1
        ticket = (eng, self.cnt[eng], eng)
        for b in rb:
            b.readers.append(ticket)
        for b in wb:
            b.writer = ticket
            b.readers = []
        self.ops[eng].append((waits, fn, self.sem[eng], 1))
        return ticket

    def emit_now(self, item):
        eng, fn, reads, writes = item
        waits, rb, wb = self._collect(eng, reads, writes)
        waits = self.pending[eng] + waits
        self.pending[eng] = []
        self.cnt[eng] += 1
        ticket = (eng, self.cnt[eng], eng)
        for b in rb:
            b.readers.append(ticket)
        for b in wb:
            b.writer = ticket
            b.readers = []
        self.ops[eng].append((waits, fn, self.sem[eng], 1))
        return ticket

    def dma(self, q, out, in_, reads=(), writes=(), key=None):
        assert key is not None
        if key not in self.dsem:
            self.dsem[key] = self.ctx.enter_context(self.nc.semaphore(f"D{key}"))
            self.dcnt[key] = 0
        waits, rb, wb = self._collect(q, reads, writes)
        waits = self.pending[q] + waits
        self.pending[q] = []
        self.dcnt[key] += 16
        ticket = (f"D{key}", self.dcnt[key], "dma")
        for b in rb:
            b.readers.append(ticket)
        for b in wb:
            b.writer = ticket
            b.readers = []

        def fn(eng, out=out, in_=in_):
            return eng.dma_start(out=out, in_=in_)

        self.ops[q].append((waits, fn, self.dsem[key], 16))
        return ticket

    def barrier(self):
        waits = []
        for e in ENGINES:
            if self.cnt[e]:
                waits.append((e, self.cnt[e]))
        for k, v in self.dcnt.items():
            waits.append((f"D{k}", v))
        for e in ENGINES:
            weng = self.waited[e]
            pend = []
            for sk, val in waits:
                if weng.get(sk, 0) < val:
                    pend.append((sk, val))
                    weng[sk] = val
            self.pending[e] = self.pending[e] + pend
        self.bufs = {}

    def _sem_obj(self, sk):
        if isinstance(sk, str) and len(sk) > 1 and sk.startswith("D"):
            return self.dsem[sk[1:]]
        return self.sem[sk]

    def finalize_into(self, block):
        P = self

        def runner(eng_name):
            def run(engobj):
                for waits, fn, inc_sem, inc_amt in P.ops[eng_name]:
                    for sk, val in waits:
                        engobj.wait_ge(P._sem_obj(sk), val)
                    fn(engobj).then_inc(inc_sem, inc_amt)
            return run

        block.sync(runner("y"))
        block.tensor(runner("t"))
        block.vector(runner("v"))
        block.scalar(runner("s"))
        block.gpsimd(runner("g"))

    def final_wait(self):
        self.barrier()
        waits = self.pending["y"]
        self.pending["y"] = []
        self.ops["y"].append((waits, lambda eng: eng.nop(), self.sem["y"], 1))



PHASES = ("ln1p", "p0", "ln1", "p1", "p2", "gate", "ffn")


def build_core_program(upto="ffn", dev=False):
    nc = bass.Bass("TRN2", target_bir_lowering=False)
    upto_i = PHASES.index(upto)
    scratch_kind = "ExternalOutput" if dev else "Internal"

    dr = {}

    def din(name, shape, dt):
        dr[name] = nc.dram_tensor(name, shape, dt, kind="ExternalInput")
        return dr[name]

    def dscr(name, shape, dt):
        dr[name] = nc.dram_tensor(name, shape, dt, kind=scratch_kind)
        return dr[name]

    din("xT", [H, TC], F32)
    din("xPT", [H, TC], F32)
    din("wq", [16, 128, 2048], F32)
    din("wf", [16, 128, 2048], F32)
    din("wi", [16, 128, 2048], F32)
    din("wo", [16, 128, 2048], F32)
    din("gwv", [128, 16], F32)
    din("gate", [128, 128], F32)
    din("gup", [10, NPAIR, 128, 4096], F32)
    din("dwn", [10, 128, NFKT * 2048], F32)
    din("ident", [128, 128], F32)
    din("smask", [128, 64], F32)
    din("selc", [8, 1024], F32)

    o_scr = dscr("o_scr", [H, TC], BF16)
    ao_scr = dscr("ao_scr", [H, TC], F32)
    x1_scr = dscr("x1_scr", [H, TC], F32)
    xn3_scr = dscr("xn3_scr", [H, TC], BF16)
    out = nc.dram_tensor("outT", [H, TC], F32, kind="ExternalOutput")
    dr["outT"] = out

    with ExitStack() as ctx:
        P = Prog(nc, ctx)

        _uid = [0]

        def sb(name, shape, dt, c=ctx):
            _uid[0] += 1
            return c.enter_context(nc.sbuf_tensor(f"{name}_{_uid[0]}", shape, dt))

        def ps(name, shape, dt, c=ctx):
            _uid[0] += 1
            return c.enter_context(nc.psum_tensor(f"{name}_{_uid[0]}", shape, dt))

        def dump(name, src_sb, shape, dt=F32):
            if not dev:
                return
            d = nc.dram_tensor(f"dbg_{name}", shape, dt, kind="ExternalOutput")
            P.dma("y", d.ap(), src_sb, reads=[src_sb.tensor if hasattr(src_sb, "tensor") else src_sb],
                  key=f"dbg_{name}")

        ones128 = sb("ones128", [128, 128], BF16)
        idb = sb("idb", [128, 128], BF16)
        idf = sb("idf", [128, 128], F32)
        smk = sb("smk", [128, 64], BF16)
        gwvb = sb("gwvb", [128, 16], F32)

        P.dma("g", idb[:], dr["ident"].ap(), writes=[idb], key="consts")
        P.dma("y", idf[:], dr["ident"].ap(), writes=[idf], key="c2")
        P.dma("g", smk[:], dr["smask"].ap(), writes=[smk], key="consts")
        P.dma("y", gwvb[:], dr["gwv"].ap(), writes=[gwvb], key="c2")
        P.op("v", lambda v: v.memset(ones128[:], 1.0), writes=[ones128])
        scanenv = {}

        def emit_ln_to_bf16(tag, src_name, dst_sb, c):
            src_dram = dr[src_name]
            pSum = ps(f"pSum_{tag}", [128, 2048], F32, c)
            pSq = ps(f"pSq_{tag}", [128, 2048], F32, c)
            xsl = [sb(f"xsl_{tag}{i}", [128, 2048], F32, c) for i in range(2)]
            xbsl = [sb(f"xb_{tag}{i}", [128, 2048], BF16, c) for i in range(2)]
            xsq = [sb(f"xq_{tag}{i}", [128, 2048], BF16, c) for i in range(2)]
            musb = sb(f"mu_{tag}", [128, 2048], F32, c)
            rsb = sb(f"r_{tag}", [128, 2048], F32, c)
            tmp = sb(f"tmp_{tag}", [128, 2048], F32, c)
            t1 = [sb(f"t1_{tag}{i}", [128, 512], F32, c) for i in range(2)]

            for kt in range(NKT):
                sl, xb, xq = xsl[kt % 2], xbsl[kt % 2], xsq[kt % 2]
                P.dma("y", sl[:], src_dram.ap()[kt * 128:(kt + 1) * 128, :],
                      reads=[src_dram], writes=[sl], key=f"x{tag}{kt % 2}")
                P.op("g", lambda g, sl=sl, xb=xb: g.tensor_copy(xb[:], sl[:]),
                     reads=[sl], writes=[xb])
                P.op("s", lambda s, xb=xb, xq=xq: s.activation(xq[:], xb[:], AF.Square),
                     reads=[xb], writes=[xq])
                for ch in range(4):
                    c0 = ch * 512
                    P.op("t", lambda t, kt=kt, c0=c0, xb=xb: t.matmul(
                        pSum[:, c0:c0 + 512], ones128[:], xb[:, c0:c0 + 512],
                        start=(kt == 0), stop=(kt == NKT - 1)),
                        reads=[ones128, xb], writes=[pSum])
                    P.op("t", lambda t, kt=kt, c0=c0, xq=xq: t.matmul(
                        pSq[:, c0:c0 + 512], ones128[:], xq[:, c0:c0 + 512],
                        start=(kt == 0), stop=(kt == NKT - 1)),
                        reads=[ones128, xq], writes=[pSq])
            P.op("v", lambda v: v.tensor_scalar(musb[:], pSum[:], 1.0 / H, None, ALU.mult),
                 reads=[pSum], writes=[musb])
            P.op("s", lambda s: s.activation(tmp[:], musb[:], AF.Square),
                 reads=[musb], writes=[tmp])
            P.op("v", lambda v: v.scalar_tensor_tensor(tmp[:], pSq[:], 1.0 / H, tmp[:],
                                                       ALU.mult, ALU.subtract),
                 reads=[pSq, tmp], writes=[tmp])
            P.op("v", lambda v: v.tensor_scalar(tmp[:], tmp[:], float(EPS), None, ALU.add),
                 reads=[tmp], writes=[tmp])
            P.op("s", lambda s: s.activation(tmp[:], tmp[:], AF.Ln),
                 reads=[tmp], writes=[tmp])
            P.op("s", lambda s: s.activation(rsb[:], tmp[:], AF.Exp, scale=-0.5),
                 reads=[tmp], writes=[rsb])
            for kt in range(NKT):
                sl = xsl[kt % 2]
                P.dma("y", sl[:], src_dram.ap()[kt * 128:(kt + 1) * 128, :],
                      reads=[src_dram], writes=[sl], key=f"x{tag}{kt % 2}")
                for ch in range(4):
                    c0 = ch * 512
                    tb = t1[ch % 2]
                    P.op("v", lambda v, sl=sl, c0=c0, tb=tb: v.tensor_tensor(
                        tb[:], sl[:, c0:c0 + 512], musb[:, c0:c0 + 512], ALU.subtract),
                        reads=[sl, musb], writes=[tb])
                    P.op("g", lambda g, kt=kt, c0=c0, tb=tb: g.tensor_tensor(
                        dst_sb[:, kt * 2048 + c0: kt * 2048 + c0 + 512],
                        tb[:], rsb[:, c0:c0 + 512], ALU.mult),
                        reads=[tb, rsb], writes=[dst_sb])

        def emit_scan_phase(main, c):
            xnb = scanenv["xnb"]
            h0b = scanenv["h0b"]
            onesf = scanenv["onesf"]
            C = CP0 if not main else CP1
            NCH = NCH0 if not main else NCH1
            NPC = NCH // 2 if main else NCH

            pPJ = [ps(f"pPJ{i}", [128, 512], F32, c) for i in range(2)]
            pU = [ps(f"pU{i}", [128, 512], F32, c) for i in range(2)]
            pTr = [ps(f"pTr{i}", [128, 1024], BF16, c) for i in range(2)]
            if main:
                pS = [ps(f"pS{i}", [128, 512], F32, c) for i in range(2)]
                pO = pS[0]

            wnames = ("wq", "wf", "wi") if main else ("wf", "wi")
            wbuf = {n: [sb(f"wb_{n}{i}", [128, 2048], BF16, c) for i in range(2)]
                    for n in wnames}

            fsb = [sb(f"fsb{i}", [128, 2048], F32, c) for i in range(2)]
            gsb = [sb(f"gsb{i}", [128, 2048], F32, c) for i in range(2)]
            esb1 = sb("esb", [128, 2048], F32, c)
            esb = [esb1, esb1]
            tmps = [sb(f"tmps{i}", [128, 2048], F32, c) for i in range(2)]
            ksb = sb("ksb", [128, 2048], BF16, c)
            vsb = [sb(f"vsb{i}", [128, 2048], BF16, c) for i in range(2)]
            khat = [sb(f"khat{i}", [128, 2048], BF16, c) for i in range(2)]
            eD = [sb(f"eD{i}", [128, 32], F32, c) for i in range(2)]
            Bc = sb("Bc", [128, 32], F32, c)
            Bp = sb("Bp", [128, 32], F32, c)
            vt_all = sb("vt_all", [128, 2048], BF16, c)
            ovtags = [[object() for _ in range(5)] for _ in range(2)]
            hB = sb("hB", [128, 128], F32, c)
            if main:
                qb = [sb(f"qb{i}", [128, 2048], BF16, c) for i in range(2)]
                qtl = [sb(f"qtl{i}", [128, 2048], BF16, c) for i in range(2)]
                Hst = sb("Hst", [128, NCH1 * 128], BF16, c)

            def load_w(n, h):
                P.dma("g", wbuf[n][h % 2][:], dr[n].ap()[h],
                      reads=[dr[n]], writes=[wbuf[n][h % 2]], key=f"w{n}{h % 2}")

            for n in wnames:
                load_w(n, 0)
                load_w(n, 1)

            def proj(wn, h, evac_dst):
                wb = wbuf[wn][h % 2]
                for grp in range(4):
                    pt = pPJ[grp % 2]
                    t0 = grp * 512
                    for kt in range(NKT):
                        P.op("t", lambda t, kt=kt, t0=t0, wb=wb, pt=pt: t.matmul(
                            pt[:], wb[:, kt * 128:(kt + 1) * 128],
                            xnb[:, kt * 2048 + t0: kt * 2048 + t0 + 512],
                            start=(kt == 0), stop=(kt == NKT - 1)),
                            reads=[wb, xnb], writes=[pt])
                    if grp % 2 == 0:
                        P.op("s", lambda s, pt=pt, t0=t0: s.copy(
                            evac_dst[:, t0:t0 + 512], pt[:]),
                            reads=[pt], writes=[evac_dst])
                    else:
                        P.op("v", lambda v, pt=pt, t0=t0: v.tensor_copy(
                            evac_dst[:, t0:t0 + 512], pt[:]),
                            reads=[pt], writes=[evac_dst])
                if h + 2 < NH:
                    load_w(wn, h + 2)

            def logsig(dst, src, hb, extra_writes=()):
                tp, ep = tmps[hb], esb[hb]
                P.op("v", lambda v: v.scalar_tensor_tensor(tp[:], src[:], -1.0, src[:],
                                                           ALU.mult, ALU.max),
                     reads=[src], writes=[tp] + list(extra_writes))
                P.op("s", lambda s: s.activation(ep[:], tp[:], AF.Exp, scale=-1.0),
                     reads=[tp], writes=[ep])
                P.op("s", lambda s: s.activation(ep[:], ep[:], AF.Ln, bias=1.0),
                     reads=[ep], writes=[ep])
                P.op("v", lambda v: v.scalar_tensor_tensor(dst[:], src[:], 0.0, ep[:],
                                                           ALU.min, ALU.subtract),
                     reads=[src, ep], writes=[dst])

            def emit_proj_elem(h, proj_now=True):
                hb = h % 2
                fs, gs, ep, tp = fsb[hb], gsb[hb], esb[hb], tmps[hb]
                Aw = fs
                subc = gs
                if proj_now:
                    proj("wf", h, fs)
                    proj("wi", h, vsb[hb])
                    if main:
                        proj("wq", h, qb[hb])
                    return
                P.op("s", lambda s: s.activation(ep[:], fs[:], AF.Sigmoid),
                     reads=[fs], writes=([ep] + list(ovtags[hb])))
                if main:
                    P.op("s", lambda s, hb=hb: s.activation(qb[hb][:], qb[hb][:],
                                                            AF.Silu),
                         reads=[qb[hb]], writes=[qb[hb]])
                P.op("s", lambda s: s.activation(gs[:], ep[:], AF.Ln),
                     reads=[ep], writes=[gs])
                P.op("v", lambda v: v.tensor_scalar(ksb[:], ep[:], -1.0, 1.0,
                                                    ALU.mult, ALU.add),
                     reads=[ep], writes=[ksb])
                P.op("v", lambda v: v.tensor_tensor_scan(Aw[:], onesf[:], gs[:], 0.0,
                                                         ALU.mult, ALU.add),
                     reads=[onesf, gs], writes=[Aw])
                P.op("v", lambda v: v.tensor_copy(Bc[:, 0:NCH], Aw[:, C - 1::C]),
                     reads=[Aw], writes=[Bc])
                P.op("v", lambda v: v.memset(Bp[:, 0:1], 0.0), writes=[Bp])
                P.op("v", lambda v: v.tensor_copy(Bp[:, 1:NCH], Bc[:, 0:NCH - 1]),
                     reads=[Bc], writes=[Bp])
                P.op("v", lambda v, hb=hb: v.tensor_tensor(eD[hb][:, 0:NCH], Bc[:, 0:NCH],
                                                           Bp[:, 0:NCH], ALU.subtract),
                     reads=[Bc, Bp], writes=[eD[hb]])
                P.op("s", lambda s, hb=hb: s.activation(eD[hb][:, 0:NCH], eD[hb][:, 0:NCH],
                                                        AF.Exp),
                     reads=[eD[hb]], writes=[eD[hb]])
                if not main:
                    for cch in range(NCH):
                        P.op("v", lambda v, cch=cch: v.tensor_scalar(
                            tp[:, cch * C:(cch + 1) * C], Aw[:, cch * C:(cch + 1) * C],
                            Bc[:, cch:cch + 1], None, ALU.subtract),
                            reads=[Aw, Bc], writes=[tp])
                    P.op("s", lambda s: s.activation(ep[:], tp[:], AF.Exp, scale=-1.0),
                         reads=[tp], writes=[ep])
                    P.op("v", lambda v, hb=hb: v.tensor_tensor(khat[hb][:], ksb[:], ep[:],
                                                               ALU.mult),
                         reads=[ksb, ep], writes=[khat[hb]])
                else:
                    for cch in range(NCH):
                        P.op("v", lambda v, cch=cch: v.tensor_scalar(
                            subc[:, cch * C:(cch + 1) * C], Aw[:, cch * C:(cch + 1) * C],
                            Bp[:, cch:cch + 1], None, ALU.subtract),
                            reads=[Aw, Bp], writes=[subc])
                    P.op("s", lambda s: s.activation(tp[:], subc[:], AF.Exp),
                         reads=[subc], writes=[tp])
                    P.op("v", lambda v, hb=hb: v.scalar_tensor_tensor(
                        qtl[hb][:], qb[hb][:], float(DK ** -0.5), tp[:],
                        ALU.mult, ALU.mult),
                        reads=[qb[hb], tp], writes=[qtl[hb]])
                    P.op("v", lambda v: v.tensor_scalar(subc[:], subc[:], -80.0, None,
                                                        ALU.max),
                         reads=[subc], writes=[subc])
                    P.op("s", lambda s: s.activation(ep[:], subc[:], AF.Exp, scale=-1.0),
                         reads=[subc], writes=[ep])
                    P.op("v", lambda v, hb=hb: v.tensor_tensor(khat[hb][:], ksb[:],
                                                               ep[:], ALU.mult),
                         reads=[ksb, ep], writes=[khat[hb]])

            def emit_ktl(h):
                hb = h % 2
                tbits = tmps[hb][:].bitcast(BF16)
                ktlv = tbits[:, 2048:4096]
                tagKL = ovtags[hb][4]
                for cch in range(NCH):
                    P.op("v", lambda v, cch=cch, hb=hb, ktlv=ktlv: v.tensor_scalar(
                        ktlv[:, cch * C:(cch + 1) * C],
                        khat[hb][:, cch * C:(cch + 1) * C],
                        eD[hb][:, cch:cch + 1], None, ALU.mult),
                        reads=[khat[hb], eD[hb]], writes=[tagKL])

            def emit_scan_part(h, pump=lambda: None):
                hb = h % 2
                hsl = h0b[:, h * 128:(h + 1) * 128]
                tpb = tmps[hb]
                tbits = tpb[:].bitcast(BF16)
                hupd = tpb[:, 0:128]
                ktv = [tbits[:, 256:384], tbits[:, 384:512]]
                ostg = tbits[:, 512:1024]
                sm_all = tbits[:, 1024:2048] if main else None
                ktlv = tbits[:, 2048:4096]
                tagK, tagS, tagO, tagH, tagKL = ovtags[hb]
                if main:
                    P.op("g", lambda g, hsl=hsl: g.tensor_copy(Hst[:, 0:128], hsl),
                         reads=[h0b], writes=[Hst])
                else:
                    P.op("v", lambda v, hsl=hsl: v.memset(hsl, 0.0), writes=[h0b])
                def emit_pair_transposes(p):
                    c0 = p * 128
                    pt = pTr[p % 2]
                    P.op("t", lambda t, c0=c0, pt=pt, hb=hb: t.transpose(
                        pt[:, 0:128], vsb[hb][:, c0:c0 + 128], idb[:]),
                        reads=[vsb[hb], idb], writes=[pt])
                    ktsrc = ktlv[:, c0:c0 + 128] if main else khat[hb][:, c0:c0 + 128]
                    ktr = tagKL if main else khat[hb]
                    P.op("t", lambda t, pt=pt, ktsrc=ktsrc: t.transpose(
                        pt[:, 128:256], ktsrc, idb[:]),
                        reads=[ktr, idb], writes=[pt])
                    P.op("s", lambda s, c0=c0, pt=pt: s.copy(
                        vt_all[:, c0:c0 + 128], pt[:, 0:128]),
                        reads=[pt], writes=[vt_all])
                    P.op("s", lambda s, p=p, pt=pt: s.copy(
                        ktv[p % 2], pt[:, 128:256]),
                        reads=[pt], writes=[tagK])

                def emit_pair_su(p):
                    if main:
                        psx = pS[p % 2]
                        for sc_ in range(2):
                            cch = p * 2 + sc_
                            cc0 = cch * C
                            b0 = sc_ * 64
                            P.op("t", lambda t, cc0=cc0, b0=b0, psx=psx, hb=hb: t.matmul(
                                psx[b0:b0 + 64, 0:64], khat[hb][:, cc0:cc0 + 64],
                                qtl[hb][:, cc0:cc0 + 64], start=True, stop=True),
                                reads=[khat[hb], qtl[hb]], writes=[psx])
                        P.op("v", lambda v, p=p, psx=psx: v.tensor_tensor(
                            sm_all[:, p * 64:(p + 1) * 64], psx[:, 0:64],
                            smk[:], ALU.mult),
                            reads=[psx, smk], writes=[tagS])
                    if main:
                        for sc_ in range(2):
                            cch = p * 2 + sc_
                            pux = pU[cch % 2]
                            b0 = sc_ * 64
                            P.op("t", lambda t, p=p, b0=b0, pux=pux: t.matmul(
                                pux[:, 0:128],
                                ktv[p % 2][b0:b0 + 64, :],
                                vt_all[b0:b0 + 64, p * 128:(p + 1) * 128],
                                start=True, stop=True),
                                reads=[tagK, vt_all], writes=[pux])
                            hsrc = hsl if cch % 2 == 0 else hB[:]
                            hdst = hB[:] if cch % 2 == 0 else hsl
                            rsrc = h0b if cch % 2 == 0 else hB
                            rdst = hB if cch % 2 == 0 else h0b
                            P.op("v", lambda v, hsrc=hsrc, hdst=hdst, cch=cch, hb=hb, pux=pux:
                                 v.scalar_tensor_tensor(hdst, hsrc,
                                                        eD[hb][:, cch:cch + 1],
                                                        pux[:, 0:128], ALU.mult, ALU.add),
                                 reads=[rsrc, eD[hb], pux], writes=[rdst])
                            if cch + 1 < NCH:
                                P.op("g", lambda g, hdst=hdst, cch=cch: g.tensor_copy(
                                    Hst[:, (cch + 1) * 128:(cch + 2) * 128], hdst),
                                    reads=[rdst], writes=[Hst])
                    else:
                        cch = p
                        pux = pU[cch % 2]
                        P.op("t", lambda t, p=p, pux=pux: t.matmul(
                            pux[:, 0:128], ktv[p % 2],
                            vt_all[:, p * 128:(p + 1) * 128],
                            start=True, stop=True),
                            reads=[tagK, vt_all], writes=[pux])
                        P.op("v", lambda v, hsl=hsl, cch=cch, pux=pux, hb=hb: v.scalar_tensor_tensor(
                            hsl, hsl, eD[hb][:, cch:cch + 1], pux[:, 0:128],
                            ALU.mult, ALU.add),
                            reads=[h0b, eD[hb], pux], writes=[h0b])

                emit_pair_transposes(0)
                for p in range(NPC):
                    if p + 1 < NPC:
                        emit_pair_transposes(p + 1)
                    emit_pair_su(p)
                if main:
                    for cch in range(NCH):
                        if cch % 4 == 0:
                            pump()
                        cc0 = cch * C
                        oslot = (cch % 8) * 64
                        pr = cch // 2
                        b0 = (cch % 2) * 64
                        P.op("t", lambda t, cch=cch, cc0=cc0, oslot=oslot, hb=hb: t.matmul(
                            pO[:, oslot:oslot + 64], Hst[:, cch * 128:(cch + 1) * 128],
                            qtl[hb][:, cc0:cc0 + 64], start=True, stop=False),
                            reads=[Hst, qtl[hb]], writes=[pO])
                        P.op("t", lambda t, b0=b0, pr=pr, oslot=oslot: t.matmul(
                            pO[:, oslot:oslot + 64], vt_all[b0:b0 + 64, pr * 128:(pr + 1) * 128],
                            sm_all[b0:b0 + 64, pr * 64:(pr + 1) * 64],
                            start=False, stop=True),
                            reads=[vt_all, tagS], writes=[pO])
                        if cch % 8 == 7:
                            e0 = (cch // 8) * 512
                            P.op("s", lambda s: s.copy(ostg, pO[:]),
                                 reads=[pO], writes=[tagO])
                            P.dma("s", o_scr.ap()[h * 128:(h + 1) * 128, e0:e0 + 512],
                                  ostg, reads=[tagO], writes=[o_scr], key="ost")

            from collections import deque

            def build_chain(h):
                P.defer = []
                emit_proj_elem(h, proj_now=False)
                if main:
                    emit_ktl(h)
                lst = P.defer
                P.defer = None
                return deque(lst)

            def flush(q, k=None):
                n = len(q) if k is None else min(k, len(q))
                for _ in range(n):
                    P.emit_now(q.popleft())

            emit_proj_elem(0, proj_now=True)
            q0 = build_chain(0)
            flush(q0)
            emit_proj_elem(1, proj_now=True)
            chainq = build_chain(1)
            flush(chainq, 6)
            for h in range(NH):
                emit_scan_part(h, lambda: flush(chainq, 8))
                flush(chainq)
                if h + 2 < NH:
                    emit_proj_elem(h + 2, proj_now=True)
                    chainq = build_chain(h + 2)
            if dev:
                dump("h0b" + ("m" if main else "p"), h0b[:], [128, NH * 128])

        def emit_p2(c):
            pStA = ps("pStA", [128, 2048], F32, c)
            pW = [ps(f"pW{i}", [128, 512], F32, c) for i in range(2)]
            pStB = ps("pStB", [128, 512], F32, c)
            pStC = ps("pStC", [128, 512], F32, c)
            osq1 = sb("osq", [128, 2048], BF16, c)
            osq2 = [osq1, osq1]
            obf_all = sb("obf_all", [128, NKT * 2048], BF16, c)
            wob = sb("wob", [128, NKT * 2048], BF16, c)
            r2 = sb("r2", [128, 2048], F32, c)
            tmp = sb("tmp2", [128, 2048], F32, c)
            musb = sb("mu2", [128, 512], F32, c)
            rsb = sb("rln2", [128, 512], F32, c)
            mu3 = sb("mu3", [128, 512], F32, c)
            r3 = sb("r3", [128, 512], F32, c)
            aosl = [sb(f"aosl{i}", [128, 512], F32, c) for i in range(2)]
            wsl = [sb(f"wsl{i}", [128, 512], F32, c) for i in range(2)]
            aob = [sb(f"aob{i}", [128, 512], BF16, c) for i in range(2)]
            asq = [sb(f"asq{i}", [128, 512], BF16, c) for i in range(2)]
            x1sl = [sb(f"x1sl{i}", [128, 512], F32, c) for i in range(2)]
            x1b_all = sb("x1b_all", [128, NKT * 512], BF16, c)
            xsq = [sb(f"xsq{i}", [128, 512], BF16, c) for i in range(2)]
            t1l = [sb(f"t1b{i}", [128, 512], F32, c) for i in range(2)]
            t1 = t1l[0]
            t2 = [sb(f"t2b{i}", [128, 512], F32, c) for i in range(2)]
            xnb3 = [sb(f"xnb3{i}", [128, 512], BF16, c) for i in range(2)]
            mu2p = sb("mu2p", [128, 512], F32, c)
            rs2p = sb("rs2p", [128, 512], F32, c)

            wotag = [object() for _ in range(4)]
            obtag = [object() for _ in range(4)]
            for kt in range(NKT):
                P.dma("g", wob[:, kt * 2048:(kt + 1) * 2048], dr["wo"].ap()[kt],
                      reads=[dr["wo"]], writes=[wotag[kt % 4]], key=f"wo{kt % 4}")
                P.op("g", lambda g, kt=kt: g.tensor_scalar(
                    wob[:, kt * 2048:(kt + 1) * 2048],
                    wob[:, kt * 2048:(kt + 1) * 2048],
                    gwvb[:, kt:kt + 1], None, ALU.mult),
                    reads=[wotag[kt % 4], gwvb], writes=[wotag[kt % 4]])
            for kt in range(NKT):
                osl_ = obf_all[:, kt * 2048:(kt + 1) * 2048]
                oq = osq2[kt % 2]
                P.dma("y", osl_, o_scr.ap()[kt * 128:(kt + 1) * 128, :],
                      reads=[o_scr], writes=[obtag[kt % 4]], key=f"ord{kt % 4}")
                P.op("s", lambda s, osl_=osl_, oq=oq, kt=kt: s.activation(oq[:], osl_, AF.Square),
                     reads=[obtag[kt % 4]], writes=[oq])
                for ch in range(4):
                    c0 = ch * 512
                    P.op("t", lambda t, kt=kt, c0=c0, oq=oq: t.matmul(
                        pStA[:, c0:c0 + 512], ones128[:], oq[:, c0:c0 + 512],
                        start=(kt == 0), stop=(kt == NKT - 1)),
                        reads=[ones128, oq], writes=[pStA])
            P.op("v", lambda v: v.tensor_scalar(tmp[:], pStA[:], 1.0 / H, None, ALU.mult),
                 reads=[pStA], writes=[tmp])
            P.op("v", lambda v: v.tensor_scalar(tmp[:], tmp[:], float(EPS), None, ALU.add),
                 reads=[tmp], writes=[tmp])
            P.op("s", lambda s: s.activation(tmp[:], tmp[:], AF.Ln),
                 reads=[tmp], writes=[tmp])
            P.op("s", lambda s: s.activation(r2[:], tmp[:], AF.Exp, scale=-0.5),
                 reads=[tmp], writes=[r2])

            def stats_to_mur(mu_dst, r_dst, rpre=None, srcB=None, srcC=None,
                             srcT=()):
                sB = pStB[:] if srcB is None else srcB
                sC = pStC[:] if srcC is None else srcC
                rT = list(srcT) if srcT else ([pStB, pStC] if srcB is None else [])
                P.op("v", lambda v: v.tensor_scalar(mu_dst[:], sB, 1.0 / H,
                                                    None, ALU.mult),
                     reads=[rT[0]], writes=[mu_dst])
                if rpre is not None:
                    P.op("v", lambda v: v.tensor_tensor(mu_dst[:], mu_dst[:], rpre,
                                                        ALU.mult),
                         reads=[mu_dst], writes=[mu_dst])
                P.op("s", lambda s: s.activation(t1[:], mu_dst[:], AF.Square),
                     reads=[mu_dst], writes=[t1])
                P.op("v", lambda v: v.scalar_tensor_tensor(t1[:], sC, 1.0 / H,
                                                           t1[:], ALU.mult,
                                                           ALU.subtract)
                     if rpre is None else
                     v.tensor_scalar(t2[0][:], sC, 1.0 / H, None, ALU.mult),
                     reads=[rT[-1], t1], writes=[t1] if rpre is None else [t2[0]])
                if rpre is not None:
                    P.op("v", lambda v: v.tensor_tensor(t2[0][:], t2[0][:], rpre, ALU.mult),
                         reads=[t2[0]], writes=[t2[0]])
                    P.op("v", lambda v: v.tensor_tensor(t2[0][:], t2[0][:], rpre, ALU.mult),
                         reads=[t2[0]], writes=[t2[0]])
                    P.op("v", lambda v: v.tensor_tensor(t1[:], t2[0][:], t1[:],
                                                        ALU.subtract),
                         reads=[t1, t2[0]], writes=[t1])
                P.op("v", lambda v: v.tensor_scalar(t1[:], t1[:], float(EPS), None,
                                                    ALU.add),
                     reads=[t1], writes=[t1])
                P.op("s", lambda s: s.activation(t1[:], t1[:], AF.Ln),
                     reads=[t1], writes=[t1])
                P.op("s", lambda s: s.activation(r_dst[:], t1[:], AF.Exp, scale=-0.5),
                     reads=[t1], writes=[r_dst])

            def wo_part(ch):
                c0 = ch * 512
                for mt in range(NKT + 1):
                    if mt < NKT:
                        pm = pW[mt % 2]
                        for kt in range(NKT):
                            P.op("t", lambda t, pm=pm, mt=mt, kt=kt, c0=c0: t.matmul(
                                pm[:], wob[:, kt * 2048 + mt * 128: kt * 2048 + (mt + 1) * 128],
                                obf_all[:, kt * 2048 + c0: kt * 2048 + c0 + 512],
                                start=(kt == 0), stop=(kt == NKT - 1)),
                                reads=[wotag[kt % 4], obtag[kt % 4]], writes=[pm])
                        asl = wsl[mt % 2]
                        P.op("s", lambda s, pm=pm, asl=asl: s.copy(asl[:], pm[:]),
                             reads=[pm], writes=[asl])
                        P.dma("s", ao_scr.ap()[mt * 128:(mt + 1) * 128, c0:c0 + 512],
                              asl[:], reads=[asl], writes=[ao_scr], key="aost")
                        P.op("g", lambda g, mt=mt, asl=asl: g.tensor_copy(
                            aob[mt % 2][:], asl[:]),
                            reads=[asl], writes=[aob[mt % 2]])
                        P.op("s", lambda s, mt=mt: s.activation(
                            asq[mt % 2][:], aob[mt % 2][:], AF.Square),
                            reads=[aob[mt % 2]], writes=[asq[mt % 2]])
                    if mt > 0:
                        q = mt - 1
                        P.op("t", lambda t, q=q: t.matmul(
                            pStB[:], ones128[:], aob[q % 2][:], start=(q == 0),
                            stop=(q == NKT - 1)),
                            reads=[ones128, aob[q % 2]], writes=[pStB])
                        P.op("t", lambda t, q=q: t.matmul(
                            pStC[:], ones128[:], asq[q % 2][:], start=(q == 0),
                            stop=(q == NKT - 1)),
                            reads=[ones128, asq[q % 2]], writes=[pStC])

            def x1_part(ch):
                c0 = ch * 512
                r2c = r2[:, c0:c0 + 512]
                P.op("v", lambda v, r2c=r2c: v.reciprocal(t2[0][:], r2c),
                     reads=[r2], writes=[t2[0]])
                P.op("v", lambda v: v.tensor_tensor(mu2p[:], musb[:], t2[0][:],
                                                    ALU.mult),
                     reads=[musb, t2[0]], writes=[mu2p])
                P.op("v", lambda v, r2c=r2c: v.tensor_tensor(rs2p[:], rsb[:], r2c,
                                                             ALU.mult),
                     reads=[rsb, r2], writes=[rs2p])
                for mt in range(NKT + 1):
                    if mt < NKT:
                        asl, xsl_, tb = aosl[mt % 2], x1sl[mt % 2], t2[mt % 2]
                        xb = x1b_all[:, mt * 512:(mt + 1) * 512]
                        P.dma("y", asl[:], ao_scr.ap()[mt * 128:(mt + 1) * 128,
                                                       c0:c0 + 512],
                              reads=[ao_scr], writes=[asl], key=f"aord{mt % 2}")
                        P.dma("y", xsl_[:], dr["xT"].ap()[mt * 128:(mt + 1) * 128,
                                                          c0:c0 + 512],
                              reads=[dr["xT"]], writes=[xsl_], key=f"xrd{mt % 2}")
                        ta = t1l[mt % 2]
                        P.op("v", lambda v, asl=asl, ta=ta: v.tensor_tensor(
                            ta[:], asl[:], mu2p[:], ALU.subtract),
                            reads=[asl, mu2p], writes=[ta])
                        P.op("g", lambda g, tb=tb, ta=ta: g.tensor_tensor(tb[:], ta[:],
                                                                          rs2p[:],
                                                                          ALU.mult),
                             reads=[ta, rs2p], writes=[tb])
                        P.op("v", lambda v, xsl_=xsl_, tb=tb: v.tensor_tensor(
                            xsl_[:], xsl_[:], tb[:], ALU.add),
                            reads=[tb, xsl_], writes=[xsl_])
                        P.dma("s", x1_scr.ap()[mt * 128:(mt + 1) * 128, c0:c0 + 512],
                              xsl_[:], reads=[xsl_], writes=[x1_scr], key="x1st")
                        P.op("g", lambda g, xb=xb, xsl_=xsl_: g.tensor_copy(xb, xsl_[:]),
                             reads=[xsl_], writes=[x1b_all])
                        P.op("s", lambda s, mt=mt, xb=xb: s.activation(
                            xsq[mt % 2][:], xb, AF.Square),
                            reads=[x1b_all], writes=[xsq[mt % 2]])
                    if mt > 0:
                        q = mt - 1
                        xbq = x1b_all[:, q * 512:(q + 1) * 512]
                        P.op("t", lambda t, q=q, xbq=xbq: t.matmul(
                            pStA[:, 0:512], ones128[:], xbq, start=(q == 0),
                            stop=(q == NKT - 1)),
                            reads=[ones128, x1b_all], writes=[pStA])
                        P.op("t", lambda t, q=q: t.matmul(
                            pStA[:, 512:1024], ones128[:], xsq[q % 2][:], start=(q == 0),
                            stop=(q == NKT - 1)),
                            reads=[ones128, xsq[q % 2]], writes=[pStA])
                stats_to_mur(mu3, r3, srcB=pStA[:, 0:512], srcC=pStA[:, 512:1024],
                             srcT=[pStA, pStA])
                for mt in range(NKT):
                    xb = x1b_all[:, mt * 512:(mt + 1) * 512]
                    ab = xnb3[mt % 2]
                    ta = t1l[mt % 2]
                    P.op("v", lambda v, xb=xb, ta=ta: v.tensor_tensor(ta[:], xb, mu3[:],
                                                               ALU.subtract),
                         reads=[x1b_all, mu3], writes=[ta])
                    P.op("g", lambda g, ab=ab, ta=ta: g.tensor_tensor(ab[:], ta[:],
                                                               r3[:], ALU.mult),
                         reads=[ta, r3], writes=[ab])
                    P.dma("s", xn3_scr.ap()[mt * 128:(mt + 1) * 128, c0:c0 + 512],
                          ab[:], reads=[ab], writes=[xn3_scr], key="xn3st")

            for ch in range(5):
                if ch >= 1:
                    stats_to_mur(musb, rsb, rpre=r2[:, (ch - 1) * 512:ch * 512])
                if ch < 4:
                    wo_part(ch)
                if ch >= 1:
                    x1_part(ch - 1)

        def emit_gate(c):
            cbT = scanenv["cbT"]
            pL = ps("pL", [128, 512], F32, c)
            pT = ps("pT", [128, 512], F32, c)
            pL2 = ps("pL2", [128, 512], F32, c)
            pT2 = ps("pT2", [128, 512], F32, c)
            gwb = sb("gwb", [128, 128], BF16, c)
            xslg = [sb(f"gxsl{i}", [128, 2048], BF16, c) for i in range(4)]
            lsb = sb("lsb", [8, 2048], F32, c)
            ltok = sb("ltok", [128, 8 * 16], F32, c)
            sc1 = sb("sc1", [128, 16], F32, c)
            es = sb("es", [128, 8 * 16], F32, c)
            cbtok = sb("cbtok", [128, 8 * 16], F32, c)
            P.dma("g", gwb[:], dr["gate"].ap(), reads=[dr["gate"]], writes=[gwb],
                  key="gate")
            regions = [(pL, 0), (pL2, 0), (pT, 0), (pT2, 0)]
            for kt in range(NKT):
                xsl_ = xslg[kt % 4]
                P.dma("y", xsl_[:], xn3_scr.ap()[kt * 128:(kt + 1) * 128, :],
                      reads=[xn3_scr], writes=[xsl_], key=f"gxrd{kt % 4}")
                for ch in range(4):
                    pr_, b0 = regions[ch]
                    P.op("t", lambda t, kt=kt, ch=ch, pr_=pr_, b0=b0, xsl_=xsl_: t.matmul(
                        pr_[b0:b0 + 8, 0:512], gwb[:, kt * 8:(kt + 1) * 8],
                        xsl_[:, ch * 512:(ch + 1) * 512],
                        start=(kt == 0), stop=(kt == NKT - 1)),
                        reads=[gwb, xsl_], writes=[pr_])
            for ch in range(4):
                pr_, b0 = regions[ch]
                P.op("s", lambda s, ch=ch, pr_=pr_, b0=b0: s.copy(
                    lsb[:, ch * 512:(ch + 1) * 512], pr_[b0:b0 + 8, 0:512]),
                    reads=[pr_], writes=[lsb])
            for tt in range(16):
                P.op("t", lambda t, tt=tt: t.transpose(
                    pT[0:128, 0:8], lsb[0:8, tt * 128:(tt + 1) * 128], idf[0:8, 0:8]),
                    reads=[lsb, idf], writes=[pT])
                P.op("v", lambda v, tt=tt: v.tensor_copy(ltok[:, tt * 8:(tt + 1) * 8],
                                                         pT[0:128, 0:8]),
                     reads=[pT], writes=[ltok])
            for tt in range(16):
                sl = ltok[:, tt * 8:(tt + 1) * 8]
                e_ = es[:, tt * 8:(tt + 1) * 8]
                cb = cbtok[:, tt * 8:(tt + 1) * 8]
                s1 = sc1[:, tt:tt + 1]
                P.op("v", lambda v, sl=sl, s1=s1: v.reduce_max(s1, sl, axis=AX.X,
                                                               negate=True),
                     reads=[ltok], writes=[sc1])
                P.op("s", lambda s, sl=sl, e_=e_, s1=s1: s.activation(
                    e_, sl, AF.Exp, bias=s1),
                    reads=[ltok, sc1], writes=[es])
                P.op("v", lambda v, e_=e_, s1=s1: v.reduce_sum(s1, e_, axis=AX.X),
                     reads=[es], writes=[sc1])
                P.op("v", lambda v, s1=s1: v.reciprocal(s1, s1),
                     reads=[sc1], writes=[sc1])
                P.op("v", lambda v, e_=e_, s1=s1: v.tensor_scalar(
                    e_, e_, s1, None, ALU.mult),
                    reads=[es, sc1], writes=[es])
                P.op("v", lambda v, e_=e_, s1=s1: v.reduce_max(s1, e_, axis=AX.X),
                     reads=[es], writes=[sc1])
                P.op("v", lambda v, e_=e_, cb=cb, s1=s1: v.tensor_scalar(
                    cb, e_, s1, None, ALU.is_equal),
                    reads=[es, sc1], writes=[cbtok])
                P.op("v", lambda v, e_=e_, cb=cb, sl=sl: v.scalar_tensor_tensor(
                    sl, cb, -1e30, e_, ALU.mult, ALU.add),
                    reads=[cbtok, es], writes=[ltok])
                P.op("v", lambda v, sl=sl, s1=s1: v.reduce_max(s1, sl, axis=AX.X),
                     reads=[ltok], writes=[sc1])
                P.op("v", lambda v, sl=sl, s1=s1: v.tensor_scalar(
                    sl, sl, s1, None, ALU.is_equal),
                    reads=[ltok, sc1], writes=[ltok])
                P.op("v", lambda v, sl=sl, cb=cb: v.tensor_tensor(cb, cb, sl, ALU.add),
                     reads=[cbtok, ltok], writes=[cbtok])
                P.op("v", lambda v, e_=e_, cb=cb: v.tensor_tensor(cb, cb, e_, ALU.mult),
                     reads=[cbtok, es], writes=[cbtok])
            for tt in range(16):
                P.op("t", lambda t, tt=tt: t.transpose(
                    pT[0:8, 0:128], cbtok[:, tt * 8:(tt + 1) * 8], idf[:]),
                    reads=[cbtok, idf], writes=[pT])
                P.op("v", lambda v, tt=tt: v.tensor_copy(cbT[:, tt * 128:(tt + 1) * 128],
                                                         pT[0:8, 0:128]),
                     reads=[pT], writes=[cbT])
            if dev:
                dump("cbtok", cbtok[:], [128, 128])
                dump("ltok", es[:], [128, 128])

        def emit_ffn(c):
            cbT = scanenv["cbT"]
            """dense MoE (10 uniform expert passes incl. 2 shared halves) + LN4 + residual.

            PSUM: 6 rotating [128x512] banks for gate/up/down + 2 stats banks.
            gate/up matmul groups are ordered so Silu/act evacuation pipelines
            behind the next group's matmuls.
            """
            pF = [ps(f"pF{i}", [128, 512], F32, c) for i in range(6)]
            pStB = ps("pStB2", [128, 512], F32, c)
            pStC = ps("pStC2", [128, 512], F32, c)
            pB = pStB
            xnh = sb("xnh", [128, NKT * 1024], BF16, c)
            gub = [sb(f"gub{i}", [128, 4096], BF16, c) for i in range(2)]
            dwb = sb("dwb", [128, NFKT * 2048], BF16, c)
            act = sb("act", [128, NFKT * 1024], BF16, c)
            ysb = sb("ysb", [128, NKT * 1024], F32, c)
            cbB = sb("cbB", [128, 1024], BF16, c)
            selb = sb("selb", [8, 1024], BF16, c)
            tbf = sb("tbf", [128, 1024], BF16, c)
            ybf = [sb(f"ybf{i}", [128, 512], BF16, c) for i in range(2)]
            ysq = [sb(f"ysq{i}", [128, 512], BF16, c) for i in range(2)]
            musb = sb("mu4", [128, 512], F32, c)
            rsb = sb("r4", [128, 512], F32, c)
            t1a = sb("t1f", [128, 512], F32, c)
            t1 = [t1a, t1a]
            t2 = [sb(f"t2f{i}", [128, 512], F32, c) for i in range(2)]
            x1sl = [sb(f"x1f{i}", [128, 512], F32, c) for i in range(2)]

            P.dma("g", selb[:], dr["selc"].ap(), reads=[dr["selc"]], writes=[selb],
                  key="selc")

            def half_loads(t0):
                for kt in range(NKT):
                    P.dma("y", xnh[:, kt * 1024:(kt + 1) * 1024],
                          xn3_scr.ap()[kt * 128:(kt + 1) * 128, t0:t0 + 1024],
                          reads=[xn3_scr], writes=[xnh], key="xnh")
                P.dma("g", gub[0][:], dr["gup"].ap()[0, 0], reads=[dr["gup"]],
                      writes=[gub[0]], key="gu0")

            half_loads(0)
            for hf in range(2):
                t0 = hf * 1024
                P.op("v", lambda v: v.memset(ysb[:], 0.0), writes=[ysb])
                blk = 0
                for e in range(10):
                    routed = e < 8
                    if routed:
                        for n2 in range(2):
                            P.op("t", lambda t, e=e, n2=n2, t0=t0: t.matmul(
                                pB[:], selb[:, e * 128:(e + 1) * 128],
                                cbT[0:8, t0 + n2 * 512:t0 + (n2 + 1) * 512],
                                start=True, stop=True),
                                reads=[selb, cbT], writes=[pB])
                            P.op("s", lambda s, n2=n2: s.copy(
                                cbB[:, n2 * 512:(n2 + 1) * 512], pB[:]),
                                reads=[pB], writes=[cbB])
                    for pr in range(NPAIR):
                        geb = gub[blk % 2]
                        ne, npr = (e, pr + 1) if pr + 1 < NPAIR else (e + 1, 0)
                        if ne < 10:
                            P.dma("g", gub[(blk + 1) % 2][:], dr["gup"].ap()[ne, npr],
                                  reads=[dr["gup"]], writes=[gub[(blk + 1) % 2]],
                                  key=f"gu{(blk + 1) % 2}")
                        if pr < NFKT:
                            P.dma("g", dwb[:, pr * 2048:(pr + 1) * 2048],
                                  dr["dwn"].ap()[e][:, pr * 2048:(pr + 1) * 2048],
                                  reads=[dr["dwn"]], writes=[dwb], key="dw0")
                        for gi, (col0, pfi) in enumerate(((0, 0), (512, 1))):
                            for kt in range(NKT):
                                P.op("t", lambda t, kt=kt, col0=col0, pfi=pfi, geb=geb: t.matmul(
                                    pF[pfi][:], geb[:, kt * 256:kt * 256 + 128],
                                    xnh[:, kt * 1024 + col0: kt * 1024 + col0 + 512],
                                    start=(kt == 0), stop=(kt == NKT - 1)),
                                    reads=[geb, xnh], writes=[pF[pfi]])
                            P.op("s", lambda s, col0=col0, pfi=pfi: s.activation(
                                tbf[:, col0:col0 + 512], pF[pfi][:], AF.Silu),
                                reads=[pF[pfi]], writes=[tbf])
                        for col0, pfi in ((0, 2), (512, 3)):
                            for kt in range(NKT):
                                P.op("t", lambda t, kt=kt, col0=col0, pfi=pfi, geb=geb: t.matmul(
                                    pF[pfi][:], geb[:, kt * 256 + 128:kt * 256 + 256],
                                    xnh[:, kt * 1024 + col0: kt * 1024 + col0 + 512],
                                    start=(kt == 0), stop=(kt == NKT - 1)),
                                    reads=[geb, xnh], writes=[pF[pfi]])
                            if routed:
                                P.op("v", lambda v, col0=col0: v.tensor_tensor(
                                    tbf[:, col0:col0 + 512], tbf[:, col0:col0 + 512],
                                    cbB[:, col0:col0 + 512], ALU.mult),
                                    reads=[tbf, cbB], writes=[tbf])
                            asl_c = (pr * 1024 + col0)
                            P.op("v", lambda v, asl_c=asl_c, col0=col0, pfi=pfi: v.tensor_tensor(
                                act[:, asl_c:asl_c + 512], tbf[:, col0:col0 + 512],
                                pF[pfi][:], ALU.mult),
                                reads=[tbf, pF[pfi]], writes=[act])
                        blk += 1
                    for mt in range(NKT):
                        pa = pF[(2 * mt) % 4]
                        pb_ = pF[(2 * mt + 1) % 4]
                        for fk in range(NFKT):
                            for n2, pd in ((0, pa), (1, pb_)):
                                P.op("t", lambda t, fk=fk, mt=mt, n2=n2, pd=pd: t.matmul(
                                    pd[:], dwb[:, fk * 2048 + mt * 128: fk * 2048 + (mt + 1) * 128],
                                    act[:, fk * 1024 + n2 * 512: fk * 1024 + (n2 + 1) * 512],
                                    start=(fk == 0), stop=(fk == NFKT - 1)),
                                    reads=[dwb, act], writes=[pd])
                        for n2, pd in ((0, pa), (1, pb_)):
                            ysl = ysb[:, mt * 1024 + n2 * 512: mt * 1024 + (n2 + 1) * 512]
                            P.op("v", lambda v, ysl=ysl, pd=pd: v.tensor_tensor(
                                ysl, ysl, pd[:], ALU.add),
                                reads=[ysb, pd], writes=[ysb])
                if hf == 0:
                    half_loads(1024)
                for ch in range(2):
                    c0 = ch * 512
                    for mt in range(NKT + 1):
                        if mt < NKT:
                            ysl = ysb[:, mt * 1024 + c0: mt * 1024 + c0 + 512]
                            P.op("g", lambda g, ysl=ysl, mt=mt: g.tensor_copy(
                                ybf[mt % 2][:], ysl),
                                reads=[ysb], writes=[ybf[mt % 2]])
                            P.op("s", lambda s, mt=mt: s.activation(
                                ysq[mt % 2][:], ybf[mt % 2][:], AF.Square),
                                reads=[ybf[mt % 2]], writes=[ysq[mt % 2]])
                        if mt > 0:
                            pm = mt - 1
                            P.op("t", lambda t, pm=pm: t.matmul(
                                pStB[:], ones128[:], ybf[pm % 2][:], start=(pm == 0),
                                stop=(pm == NKT - 1)),
                                reads=[ones128, ybf[pm % 2]], writes=[pStB])
                            P.op("t", lambda t, pm=pm: t.matmul(
                                pStC[:], ones128[:], ysq[pm % 2][:], start=(pm == 0),
                                stop=(pm == NKT - 1)),
                                reads=[ones128, ysq[pm % 2]], writes=[pStC])
                    P.op("v", lambda v: v.tensor_scalar(musb[:], pStB[:], 1.0 / H, None,
                                                        ALU.mult),
                         reads=[pStB], writes=[musb])
                    P.op("s", lambda s: s.activation(t1[0][:], musb[:], AF.Square),
                         reads=[musb], writes=[t1[0]])
                    P.op("v", lambda v: v.scalar_tensor_tensor(t1[0][:], pStC[:], 1.0 / H,
                                                               t1[0][:], ALU.mult,
                                                               ALU.subtract),
                         reads=[pStC, t1[0]], writes=[t1[0]])
                    P.op("v", lambda v: v.tensor_scalar(t1[0][:], t1[0][:], float(EPS), None, ALU.add),
                         reads=[t1[0]], writes=[t1[0]])
                    P.op("s", lambda s: s.activation(t1[0][:], t1[0][:], AF.Ln),
                         reads=[t1[0]], writes=[t1[0]])
                    P.op("s", lambda s: s.activation(rsb[:], t1[0][:], AF.Exp, scale=-0.5),
                         reads=[t1[0]], writes=[rsb])
                    for mt in range(NKT):
                        ysl = ysb[:, mt * 1024 + c0: mt * 1024 + c0 + 512]
                        ta, tb, xs = t1[mt % 2], t2[mt % 2], x1sl[mt % 2]
                        P.dma("y", xs[:], x1_scr.ap()[mt * 128:(mt + 1) * 128,
                                                      t0 + c0:t0 + c0 + 512],
                              reads=[x1_scr], writes=[xs], key=f"x1rd{mt % 2}")
                        P.op("v", lambda v, ysl=ysl, ta=ta: v.tensor_tensor(
                            ta[:], ysl, musb[:], ALU.subtract),
                            reads=[ysb, musb], writes=[ta])
                        P.op("g", lambda g, ta=ta, tb=tb: g.tensor_tensor(
                            tb[:], ta[:], rsb[:], ALU.mult),
                            reads=[ta, rsb], writes=[tb])
                        P.op("v", lambda v, xs=xs, tb=tb: v.tensor_tensor(
                            xs[:], xs[:], tb[:], ALU.add),
                            reads=[tb, xs], writes=[xs])
                        P.dma("s", out.ap()[mt * 128:(mt + 1) * 128,
                                            t0 + c0:t0 + c0 + 512],
                              xs[:], reads=[xs], writes=[out], key="outst")

        with ExitStack() as cA:
            xnb = sb("xnb", [128, NKT * 2048], BF16, cA)
            h0b = sb("h0b", [128, NH * 128], F32, cA)
            onesf = sb("onesf", [128, TC], BF16, cA)
            scanenv["xnb"] = xnb
            scanenv["h0b"] = h0b
            scanenv["onesf"] = onesf
            P.op("v", lambda v: v.memset(onesf[:], 1.0), writes=[onesf])
            with ExitStack() as c1:
                emit_ln_to_bf16("lp", "xPT", xnb, c1)
                if dev:
                    dump("xnp", xnb[:, 0:2048], [128, 2048], BF16)
            P.barrier()
            if upto_i >= 1:
                with ExitStack() as c2:
                    emit_scan_phase(False, c2)
                P.barrier()
            if upto_i >= 2:
                with ExitStack() as c3:
                    emit_ln_to_bf16("l1", "xT", xnb, c3)
                    if dev:
                        dump("xn", xnb[:, 0:2048], [128, 2048], BF16)
                P.barrier()
            if upto_i >= 3:
                with ExitStack() as c4:
                    emit_scan_phase(True, c4)
                P.barrier()
        if upto_i >= 4:
            with ExitStack() as c5:
                emit_p2(c5)
            P.barrier()
        if upto_i >= 5:
            with ExitStack() as cGF:
                scanenv["cbT"] = sb("cbT", [8, TC], BF16, cGF)
                with ExitStack() as c6:
                    emit_gate(c6)
                P.barrier()
                if upto_i >= 6:
                    with ExitStack() as c7:
                        emit_ffn(c7)
                    P.barrier()

        P.final_wait()
        block = ctx.enter_context(nc.Block())
        P.finalize_into(block)

    return nc



def _host_inputs(x, Wq, Wf, Wi, Wo, g_weight, gate_weight, gup, dwn, sgup, sdwn):
    def wgroups(W):
        return np.ascontiguousarray(
            W.reshape(16, 128, 16, 128).transpose(2, 1, 0, 3).reshape(16, 128, 2048))

    wq_h = wgroups(Wq)
    wf_h = wgroups(Wf)
    wi_h = wgroups(Wi)
    wo_h = np.ascontiguousarray(Wo.reshape(16, 128, 2048))
    gwv_h = np.ascontiguousarray(g_weight.reshape(16, 128).T)
    gate_h = np.ascontiguousarray(
        gate_weight.T.reshape(16, 128, 8).transpose(1, 0, 2).reshape(128, 128))

    def pair_block(G):
        o = np.empty((NPAIR, 128, 4096), np.float32)
        for p in range(NPAIR):
            blk = np.concatenate([G[p * 128:(p + 1) * 128, :],
                                  G[FEXP + p * 128:FEXP + (p + 1) * 128, :]], axis=0)
            o[p] = blk.T.reshape(16, 128, 256).transpose(1, 0, 2).reshape(128, 4096)
        return o

    gup_h = np.empty((10, NPAIR, 128, 4096), np.float32)
    for e in range(8):
        gup_h[e] = pair_block(gup[e])
    A = np.concatenate([sgup[0:FEXP, :], sgup[2 * FEXP:3 * FEXP, :]], axis=0)
    Bb = np.concatenate([sgup[FEXP:2 * FEXP, :], sgup[3 * FEXP:4 * FEXP, :]], axis=0)
    gup_h[8] = pair_block(A)
    gup_h[9] = pair_block(Bb)

    def down_block(DT):
        return np.ascontiguousarray(
            DT.reshape(NFKT, 128, 2048).transpose(1, 0, 2).reshape(128, NFKT * 2048))

    dwn_h = np.empty((10, 128, NFKT * 2048), np.float32)
    for e in range(8):
        dwn_h[e] = down_block(dwn[e].T)
    sdT = sdwn.T
    dwn_h[8] = down_block(sdT[0:FEXP])
    dwn_h[9] = down_block(sdT[FEXP:2 * FEXP])

    ident = np.eye(128, dtype=np.float32)
    tri = np.triu(np.ones((64, 64), np.float32))
    smask_h = np.concatenate([tri, tri], axis=0)
    selc = np.zeros((8, 1024), np.float32)
    for e in range(8):
        selc[e, e * 128:(e + 1) * 128] = 1.0

    shared = dict(wq=wq_h, wf=wf_h, wi=wi_h, wo=wo_h, gwv=gwv_h, gate=gate_h,
                  gup=gup_h, dwn=dwn_h, ident=ident, smask=smask_h, selc=selc)
    in_maps = []
    zeros = np.zeros((H, TC), np.float32)
    for core in range(8):
        b, hf = core // 2, core % 2
        m = dict(shared)
        m["xT"] = np.ascontiguousarray(x[b, hf * TC:(hf + 1) * TC, :].T)
        m["xPT"] = (np.ascontiguousarray(x[b, 0:TC, :].T) if hf == 1 else zeros)
        in_maps.append(m)
    return in_maps


def kernel(L0_attn_x, L0_attn_Wq, L0_attn_Wf, L0_attn_Wi, L0_attn_Wo,
           L0_attn_g_weight, L0_ffn_gate_weight, L0_ffn_gate_up_weights,
           L0_ffn_down_weights, L0_ffn_shared_gate_up, L0_ffn_shared_down):
    in_maps = _host_inputs(
        np.asarray(L0_attn_x, np.float32), np.asarray(L0_attn_Wq, np.float32),
        np.asarray(L0_attn_Wf, np.float32), np.asarray(L0_attn_Wi, np.float32),
        np.asarray(L0_attn_Wo, np.float32), np.asarray(L0_attn_g_weight, np.float32),
        np.asarray(L0_ffn_gate_weight, np.float32),
        np.asarray(L0_ffn_gate_up_weights, np.float32),
        np.asarray(L0_ffn_down_weights, np.float32),
        np.asarray(L0_ffn_shared_gate_up, np.float32),
        np.asarray(L0_ffn_shared_down, np.float32))
    nc = build_core_program()
    res = run_bass_kernel_spmd(nc, in_maps, core_ids=list(range(8)))
    outp = np.empty((Bsz, S, H), np.float32)
    for core in range(8):
        b, hf = core // 2, core % 2
        outp[b, hf * TC:(hf + 1) * TC, :] = np.asarray(res.results[core]["outT"]).T
    return outp

